# version 08
# baseline (speedup 1.0000x reference)
import contextlib
import numpy as np
import ml_dtypes
import concourse.bass as bass
import concourse.mybir as mybir
from concourse.bass_utils import run_bass_kernel_spmd

F32 = mybir.dt.float32
BF16 = mybir.dt.bfloat16
AF = mybir.ActivationFunctionType
ALU = mybir.AluOpType
AX = mybir.AxisListType

N_CORES = 8
T = 1024
D = 4096
HEADS = 32
S = 8192
SCALE = 1.0 / float(np.sqrt(128.0))
DBG = True


def _pairs_mat():
    p = np.zeros((8, 28), np.float32)
    c = 0
    for i in range(8):
        for j in range(i + 1, 8):
            p[i, c] = 1.0
            p[j, c] = 1.0
            c += 1
    return p


class PB:
    ENGINES = ("tensor", "scalar", "vector", "gpsimd", "sync")

    def __init__(self, nc):
        self.nc = nc
        self.ops = {e: [] for e in self.ENGINES}
        self.sem = {}
        self.cnt = {}
        self.waited = {e: {} for e in self.ENGINES}
        self.wait_records = []

    def _sem(self, name):
        if name not in self.sem:
            self.sem[name] = self.nc.alloc_semaphore(name)
            self.cnt[name] = 0
        return self.sem[name]

    def wait(self, eng, sname, val):
        if val is None or val <= 0:
            return
        sem = self._sem(sname)
        w = self.waited[eng]
        if w.get(sname, -1) >= val:
            return
        w[sname] = val
        self.wait_records.append((eng, sname, val))
        self.ops[eng].append(lambda e, sem=sem, val=val: e.wait_ge(sem, val))

    def waitp(self, eng, pair):
        if pair:
            self.wait(eng, pair[0], pair[1])

    def op(self, eng, fn, inc=None, incn=1):
        if inc is None:
            self.ops[eng].append(lambda e, fn=fn: fn(e))
            return None
        sem = self._sem(inc)
        self.cnt[inc] += incn
        self.ops[eng].append(
            lambda e, fn=fn, sem=sem, n=incn: fn(e).then_inc(sem, n))
        return self.cnt[inc]

    def dma(self, eng, out_ap, in_ap, sem=None, **kw):
        sname = sem or ("g" if eng == "gpsimd" else "y")
        n = self.op(eng, lambda e, o=out_ap, i=in_ap, kw=kw: e.dma_start(out=o, in_=i, **kw),
                    inc=sname, incn=16)
        return (sname, n)

    def barrier(self):
        snap = dict(self.cnt)
        for eng in self.ENGINES:
            for sname, val in snap.items():
                self.wait(eng, sname, val)

    def finalize_check(self):
        for eng, sname, val in self.wait_records:
            assert val <= self.cnt.get(sname, 0), (
                f"{eng} waits {sname}>={val} but total incs = {self.cnt.get(sname, 0)}")


def build(nc, DBG=False):
    p = PB(nc)
    for base in ("t", "s", "v", "g", "y", "c1", "c2"):
        p._sem(base)
    xc = nc.dram_tensor("xc", [T, D], F32, kind="ExternalInput").ap()
    qkvw = nc.dram_tensor("qkvw", [D, 3 * D], F32, kind="ExternalInput").ap()
    qkvb = nc.dram_tensor("qkvb", [3 * D], F32, kind="ExternalInput").ap()
    ow = nc.dram_tensor("ow", [D, D], F32, kind="ExternalInput").ap()
    ob = nc.dram_tensor("ob", [D], F32, kind="ExternalInput").ap()
    gate_w = nc.dram_tensor("gate_w", [D, 8], F32, kind="ExternalInput").ap()
    up_w = nc.dram_tensor("up_w", [D, 8192], F32, kind="ExternalInput").ap()
    down_w = nc.dram_tensor("down_w", [8192, D], F32, kind="ExternalInput").ap()
    fc1_w = nc.dram_tensor("fc1_w", [D, 2048], F32, kind="ExternalInput").ap()
    fc2_w = nc.dram_tensor("fc2_w", [2048, D], F32, kind="ExternalInput").ap()
    expert_w = nc.dram_tensor("expert_w", [2048, 2048], F32, kind="ExternalInput").ap()
    ln_g = nc.dram_tensor("ln_g", [D], F32, kind="ExternalInput").ap()
    ln_b = nc.dram_tensor("ln_b", [D], F32, kind="ExternalInput").ap()
    out_d = nc.dram_tensor("out", [T, D], F32, kind="ExternalOutput").ap()

    q_spill = nc.dram_tensor("q_spill", [D, T], BF16, kind="Internal").ap()
    k_in = nc.dram_tensor("k_in", [D, T], BF16, kind="Internal").ap()
    v_in = nc.dram_tensor("v_in", [T, D], BF16, kind="Internal").ap()
    k_g = nc.dram_tensor("k_g", [N_CORES * D, T], BF16, kind="Internal",
                         addr_space="Shared").ap()
    v_g = nc.dram_tensor("v_g", [S, D], BF16, kind="Internal",
                         addr_space="Shared").ap()
    ctx_dram = nc.dram_tensor("ctx_dram", [D, T], BF16, kind="Internal").ap()
    x1_dram = nc.dram_tensor("x1_dram", [T, D], F32, kind="Internal").ap()
    h1_dram = nc.dram_tensor("h1_dram", [T, D], F32, kind="Internal").ap()
    h1fm_dram = nc.dram_tensor("h1fm_dram", [D, T], BF16, kind="Internal").ap()
    pre_dram = nc.dram_tensor("pre_dram", [T, D], F32, kind="Internal").ap()
    knl_d = nc.dram_tensor("knl_d", [1, 32], F32, kind="Internal").ap()
    kng_d = nc.dram_tensor("kng_d", [1, 32], F32, kind="Internal").ap()

    if DBG:
        x1_dbg = nc.dram_tensor("x1_dbg", [T, D], F32, kind="ExternalOutput").ap()
        h1_dbg = nc.dram_tensor("h1_dbg", [T, D], F32, kind="ExternalOutput").ap()
        ctx_dbg = nc.dram_tensor("ctx_dbg", [D, T], BF16, kind="ExternalOutput").ap()
        kin_dbg = nc.dram_tensor("kin_dbg", [D, T], BF16, kind="ExternalOutput").ap()
        vin_dbg = nc.dram_tensor("vin_dbg", [T, D], BF16, kind="ExternalOutput").ap()
        qsp_dbg = nc.dram_tensor("qsp_dbg", [D, T], BF16, kind="ExternalOutput").ap()
        nrm_dbg = nc.dram_tensor("nrm_dbg", [1, 128], F32, kind="ExternalOutput").ap()

    ident_d = nc.inline_tensor(np.eye(128, dtype=ml_dtypes.bfloat16), "ident_const").ap()
    pairs_d = nc.inline_tensor(_pairs_mat(), "pairs_const").ap()

    ones_f = nc.alloc_sbuf_tensor("ones_f", [128, 1], F32)
    ones_bf = nc.alloc_sbuf_tensor("ones_bf", [128, 1], BF16)
    onesr_f = nc.alloc_sbuf_tensor("onesr_f", [1, 128], F32)
    i_bf = nc.alloc_sbuf_tensor("i_bf", [128, 128], BF16)
    p_sb = nc.alloc_sbuf_tensor("p_sb", [8, 28], F32)
    bias_cols = nc.alloc_sbuf_tensor("bias_cols", [128, 96], F32)
    gate_sb = nc.alloc_sbuf_tensor("gate_sb", [128, 32 * 8], BF16)
    gsum_cols = nc.alloc_sbuf_tensor("gsum_cols", [128, 8], F32)
    cols = nc.alloc_sbuf_tensor("cols", [128, 32], F32)
    rowb = nc.alloc_sbuf_tensor("rowb", [1, 2 * 512], F32)
    bias_h = nc.alloc_sbuf_tensor("bias_h", [128, 2], F32)
    bias_const = nc.alloc_sbuf_tensor("bias_const", [128, 1], F32)
    scols = nc.alloc_sbuf_tensor("scols", [128, 8], F32)
    qn2 = nc.alloc_sbuf_tensor("qn2", [1, 64], F32)
    kn2l = nc.alloc_sbuf_tensor("kn2l", [1, 32], F32)
    kn2g = nc.alloc_sbuf_tensor("kn2g", [1, 32], F32)
    vfl = nc.alloc_sbuf_tensor("vfl", [128, 2], F32)
    sfl = nc.alloc_sbuf_tensor("sfl", [128, 2], F32)

    C_SUM, C_SQ, C_MU, C_MSQ, C_MU2, C_VAR, C_SD, C_RSTD = 0, 1, 2, 3, 4, 5, 6, 7
    C_MKA, C_MKB, C_MQ, C_C2, C_BV = 8, 9, 10, 11, 13
    C_SUMB, C_SQB = 14, 15

    def col(j):
        return cols[:, j:j + 1]

    def scol(j):
        return cols[0:1, j:j + 1]

    S_SD, S_SQ, S_C = 0, 1, 2

    def sco(j):
        return scols[:, j:j + 1]

    def vflush():
        return p.op("vector", lambda e: e.tensor_copy(vfl[:, 0:1], vfl[:, 1:2]), inc="v")

    def sflush():
        return p.op("scalar", lambda e: e.copy(sfl[:, 0:1], sfl[:, 1:2]), inc="s")

    def vchain(fn):
        n = p.op("vector", fn, inc="v")
        p.wait("vector", "v", n)
        return n

    def schain(fn):
        n = p.op("scalar", fn, inc="s")
        p.wait("scalar", "s", n)
        return n

    C_NT = 16

    def ln_stats(sumsq_ap):
        vchain(lambda e: e.tensor_scalar_mul(col(C_MU), col(C_SUM), 1.0 / D))
        vchain(lambda e, a=sumsq_ap: e.tensor_scalar_mul(col(C_MSQ), a, 1.0 / D))
        vchain(lambda e: e.tensor_tensor(col(C_MU2), col(C_MU), col(C_MU), ALU.mult))
        vchain(lambda e: e.tensor_tensor(col(C_VAR), col(C_MSQ), col(C_MU2), ALU.subtract))
        vchain(lambda e: e.tensor_scalar_add(col(C_SD), col(C_VAR), 1e-5))
        n = vflush()
        p.wait("scalar", "v", n)
        schain(lambda e: e.sqrt(sco(S_SD), col(C_SD)))
        m = sflush()
        p.wait("vector", "s", m)
        vchain(lambda e: e.reciprocal(col(C_RSTD), sco(S_SD)))
        r = C_RSTD
        for _ in range(2):
            r2 = C_NT if r == C_RSTD else C_RSTD
            vchain(lambda e, r=r: e.tensor_tensor(col(C_NT + 1), col(r), col(r), ALU.mult))
            vchain(lambda e: e.tensor_tensor(col(C_NT + 2), col(C_NT + 1), col(C_SD), ALU.mult))
            vchain(lambda e: e.tensor_scalar(col(C_NT + 1), col(C_NT + 2), -0.5, 1.5,
                                             ALU.mult, ALU.add))
            vchain(lambda e, r=r, r2=r2: e.tensor_tensor(col(r2), col(r), col(C_NT + 1), ALU.mult))
            r = r2
        assert r == C_RSTD
        return p.cnt["v"]

    p.dma("gpsimd", i_bf[:], ident_d, sem="big")
    p.dma("gpsimd", p_sb[:], pairs_d, sem="big")
    gw3 = gate_w.rearrange("(c p) e -> p c e", p=128)
    _, big_init = p.dma("gpsimd", gate_sb[:].rearrange("p (c e) -> p c e", e=8), gw3, sem="big")
    p.op("vector", lambda e: e.memset(ones_f[:], 1.0))
    p.op("vector", lambda e: e.memset(ones_bf[:], 1.0))
    p.op("vector", lambda e: e.memset(bias_const[:], -2.0))
    v_ones = p.op("vector", lambda e: e.memset(onesr_f[:], 1.0), inc="v")

    with nc.psum_tensor("pinit", [128, 512], F32) as pinit:
        s_last = 0
        t_need = 0
        for gidx in range(24):
            slot = gidx % 2
            rb = rowb[0:1, slot * 512:(slot + 1) * 512]
            p.wait("sync", "t", t_need)
            y_row = p.dma("sync", rb, qkvb[gidx * 512:(gidx + 1) * 512].rearrange("(a b) -> a b", a=1),
                          sem=f"rb{slot}")
            p.waitp("tensor", y_row)
            p.wait("tensor", "v", v_ones)
            p.wait("tensor", "s", s_last)
            for j in range(4):
                t_need = p.op("tensor", lambda e, rb=rb, j=j: e.matmul(
                    pinit[:, j:j + 1], rb[0:1, j * 128:(j + 1) * 128], ones_f[0:1, 0:1],
                    start=True, stop=True), inc="t")
            p.wait("scalar", "t", t_need)
            s_last = p.op("scalar", lambda e, gidx=gidx: e.copy(
                bias_cols[:, gidx * 4:(gidx + 1) * 4], pinit[:, 0:4]), inc="s")
        p.wait("tensor", "s", s_last)
        rowb_free_t = t_need
    p.barrier()

    og_order = list(range(32, 64)) + list(range(64, 96)) + list(range(0, 32))

    with (
        nc.sbuf_tensor("x_fm", [128, 32 * 1024], BF16) as x_fm,
        nc.sbuf_tensor("wbuf", [128, 2 * 32 * 512], BF16) as wbuf,
        nc.sbuf_tensor("x_tm", [128, 2 * 4096], BF16) as x_tm,
        nc.sbuf_tensor("stg", [128, 4 * 512], BF16) as stg,
        nc.sbuf_tensor("vstg", [128, 2 * 512], BF16) as vstg,
        nc.sbuf_tensor("nsq", [128, 2 * 512], BF16) as nsq,
        nc.psum_tensor("pnrm", [128, 1024], F32) as pnrm,
        nc.psum_tensor("pqa", [128, 1024], F32) as pqa,
        nc.psum_tensor("pqb", [128, 1024], F32) as pqb,
        nc.psum_tensor("ptr", [128, 2 * 512], F32) as ptr,
    ):
        tt_tdone = [0] * 8
        s_tr = 0
        for tt in range(8):
            slot = tt % 2
            if tt >= 2:
                p.wait("gpsimd", "t", tt_tdone[tt - 2])
            gl = p.dma("gpsimd", x_tm[:, slot * 4096:(slot + 1) * 4096],
                       xc[tt * 128:(tt + 1) * 128, :], sem=f"xl{slot}")
            p.waitp("tensor", gl)
            p.wait("tensor", "big", big_init)
            for fcg in range(8):
                pslot = ptr[:, (fcg % 2) * 512:(fcg % 2 + 1) * 512]
                p.wait("tensor", "s", s_tr)
                for j in range(4):
                    fc = fcg * 4 + j
                    tdone = p.op("tensor", lambda e, slot=slot, fc=fc, j=j, ps=pslot: e.matmul(
                        ps[:, j * 128:(j + 1) * 128],
                        x_tm[:, slot * 4096 + fc * 128: slot * 4096 + fc * 128 + 128],
                        i_bf[:], start=True, stop=True), inc="t")
                p.wait("scalar", "t", tdone)
                dst = x_fm[:].rearrange("q (c t) -> q c t", t=1024)[
                    :, fcg * 4:(fcg + 1) * 4, tt * 128:(tt + 1) * 128]
                src = pslot.rearrange("q (c t) -> q c t", t=128)
                s_tr = p.op("scalar", lambda e, d=dst, sp=src: e.copy(d, sp), inc="s")
            tt_tdone[tt] = tdone
        p.wait("tensor", "s", s_tr)

        nsq_free_t = [0, 0]
        pnrm_free_v = [0, 0]
        wslab_done = [0, 0]
        wslab_load = [None, None]
        slot_free_s = [0, 0, 0, 0]
        stg_free = [None] * 4
        vstg_free = [None, None]
        n_vstores = 0
        ptr_ev = s_tr
        psum_halves = [
            lambda: pqa[:, 0:512], lambda: pqa[:, 512:1024],
            lambda: pqb[:, 0:512], lambda: pqb[:, 512:1024],
        ]
        qkvw3 = qkvw.rearrange("(c p) o -> p c o", p=128)
        si = 0
        sgi = 0
        for grp in range(24):
            ogbase = og_order[grp * 4]
            pbuf = grp % 2
            p.wait("gpsimd", "t", wslab_done[pbuf])
            wdst = wbuf[:, pbuf * 16384:(pbuf + 1) * 16384].rearrange("p (c o) -> p c o", o=512)
            wsrc = qkvw3[:, :, ogbase * 128:ogbase * 128 + 512]
            wslab_load[pbuf] = p.dma("gpsimd", wdst, wsrc, sem=f"w{pbuf}")

            for og_in in range(4):
                og = ogbase + og_in
                for th in range(2):
                    half = psum_halves[si % 4]()
                    p.waitp("tensor", wslab_load[pbuf])
                    p.wait("tensor", "s", slot_free_s[si % 4])
                    for fc in range(32):
                        lt = wbuf[:, pbuf * 16384 + fc * 512 + og_in * 128:
                                  pbuf * 16384 + fc * 512 + og_in * 128 + 128]
                        rh = x_fm[:, fc * 1024 + th * 512: fc * 1024 + (th + 1) * 512]
                        tq = p.op("tensor", lambda e, h=half, lt=lt, rh=rh, f=fc: e.matmul(
                            h, lt, rh, start=(f == 0), stop=(f == 31)), inc="t")
                    wslab_done[pbuf] = tq
                    p.wait("scalar", "t", tq)
                    if og < 64:
                        sem_pfx = "sk" if og >= 32 else "sq"
                        p.waitp("scalar", stg_free[sgi % 4])
                        dstg = stg[:, (sgi % 4) * 512:(sgi % 4 + 1) * 512]
                        sev = p.op("scalar", lambda e, d=dstg, h=half, og=og: e.activation(
                            d, h, AF.Identity, bias=bias_cols[:, og:og + 1], scale=1.0),
                            inc="s")
                        slot_free_s[si % 4] = sev
                        p.wait("sync", "s", sev)
                        if og < 32:
                            tgt = q_spill[og * 128:(og + 1) * 128, th * 512:(th + 1) * 512]
                        else:
                            tgt = k_in[(og - 32) * 128:(og - 32) * 128 + 128, th * 512:(th + 1) * 512]
                        stg_free[sgi % 4] = p.dma("sync", tgt, dstg, sem=f"{sem_pfx}{sgi % 4}")
                        ns = sgi % 2
                        p.wait("scalar", "t", nsq_free_t[ns])
                        snq = p.op("scalar", lambda e, d=dstg, ns=ns: e.activation(
                            nsq[:, ns * 512:(ns + 1) * 512], d, AF.Square), inc="s")
                        p.wait("tensor", "s", snq)
                        p.wait("tensor", "v", pnrm_free_v[ns])
                        tnq = p.op("tensor", lambda e, ns=ns: e.matmul(
                            pnrm[0:1, ns * 512:(ns + 1) * 512], ones_bf[:],
                            nsq[:, ns * 512:(ns + 1) * 512], start=True, stop=True), inc="t")
                        nsq_free_t[ns] = tnq
                        p.wait("vector", "t", tnq)
                        if og < 32:
                            dstn = qn2[0:1, og * 2 + th: og * 2 + th + 1]
                            pnrm_free_v[ns] = p.op("vector", lambda e, d=dstn, ns=ns: e.tensor_reduce(
                                d, pnrm[0:1, ns * 512:(ns + 1) * 512], AX.X, ALU.max), inc="v")
                        else:
                            dstn = cols[0:1, 20 + th: 21 + th]
                            vr_ = p.op("vector", lambda e, d=dstn, ns=ns: e.tensor_reduce(
                                d, pnrm[0:1, ns * 512:(ns + 1) * 512], AX.X, ALU.max), inc="v")
                            pnrm_free_v[ns] = vr_
                            if th == 1:
                                p.wait("vector", "v", vr_)
                                vchain(lambda e, hh=og - 32: e.tensor_tensor(
                                    kn2l[0:1, hh:hh + 1], cols[0:1, 20:21],
                                    cols[0:1, 21:22], ALU.max))
                        sgi += 1
                    else:
                        vslot = n_vstores % 2
                        p.waitp("scalar", vstg_free[vslot])
                        vs1 = vstg[:, vslot * 512:(vslot + 1) * 512]
                        sev = p.op("scalar", lambda e, d=vs1, h=half, og=og: e.activation(
                            d, h, AF.Identity, bias=bias_cols[:, og:og + 1], scale=1.0),
                            inc="s")
                        slot_free_s[si % 4] = sev
                        p.wait("tensor", "s", sev)
                        ptslot = ptr[:, vslot * 512:(vslot + 1) * 512]
                        p.wait("tensor", "s", ptr_ev)
                        for j in range(4):
                            ttr = p.op("tensor", lambda e, ps=ptslot, j=j, vs=vs1: e.matmul(
                                ps[:, j * 128:(j + 1) * 128], vs[:, j * 128:(j + 1) * 128],
                                i_bf[:], start=True, stop=True), inc="t")
                        p.wait("scalar", "t", ttr)
                        ptr_ev = p.op("scalar", lambda e, d=vs1, ps=ptslot: e.copy(d, ps), inc="s")
                        p.wait("sync", "s", ptr_ev)
                        vdst = v_in[th * 512:(th + 1) * 512, (og - 64) * 128:(og - 64) * 128 + 128]
                        vdst3 = vdst.rearrange("(j q) d -> q j d", q=128)
                        vsrc3 = vs1.rearrange("q (j d) -> q j d", d=128)
                        vstg_free[vslot] = p.dma("sync", vdst3, vsrc3, sem=f"sv{vslot}")
                        n_vstores += 1
                    si += 1
            if grp == 15:
                for j in range(2):
                    p.wait("gpsimd", f"sv{j}", p.cnt.get(f"sv{j}", 0))
                p.op("gpsimd", lambda e: e.collective_compute(
                    "AllGather", ALU.bypass, replica_groups=[list(range(N_CORES))],
                    ins=[v_in], outs=[v_g]), inc="c2")
            if grp == 9:
                for j in range(4):
                    p.wait("gpsimd", f"sk{j}", p.cnt.get(f"sk{j}", 0))
                p.op("gpsimd", lambda e: e.collective_compute(
                    "AllGather", ALU.bypass, replica_groups=[list(range(N_CORES))],
                    ins=[k_in], outs=[k_g]), inc="c1")
                vkn = vflush()
                p.wait("sync", "v", vkn)
                ykn = p.dma("sync", knl_d, kn2l[0:1, :], sem="kn")
                p.waitp("gpsimd", ykn)
                p.op("gpsimd", lambda e: e.collective_compute(
                    "AllReduce", ALU.max, replica_groups=[list(range(N_CORES))],
                    ins=[knl_d], outs=[kng_d]), inc="c3")
    p.barrier()

    with (
        nc.sbuf_tensor("kbuf", [128, 2 * 8192], BF16) as kbuf,
        nc.sbuf_tensor("vbuf", [128, 2 * 8192], BF16) as vbuf,
        nc.sbuf_tensor("qb", [128, 2 * 512], BF16) as qb,
        nc.sbuf_tensor("prb", [128, 2 * 1024], BF16) as prb,
        nc.sbuf_tensor("pssum", [128, 512], BF16) as pssum,
        nc.sbuf_tensor("lpp", [128, 2 * 512], F32) as lpp,
        nc.sbuf_tensor("rcp", [1, 512], F32) as rcp,
        nc.sbuf_tensor("rbb", [128, 512], F32) as rbb,
        nc.sbuf_tensor("ctxs", [128, 2 * 512], BF16) as ctxs,
        nc.psum_tensor("pa", [128, 1024], F32) as pa,
        nc.psum_tensor("pb", [128, 1024], F32) as pb,
        nc.psum_tensor("pc", [128, 1024], F32) as pc,
        nc.psum_tensor("pd", [128, 512], F32) as pd,
    ):
        kv_used_t = [0, 0]
        kv_load = [None, None]
        qb_free = [0, 0]
        ctxs_free = [None, None]
        pc_free = [0, 0]
        pd_guard_v = 0
        unit = 0
        last_exp = [0, 0]
        last_pv = [0, 0]
        last_ps = [0, 0]
        pd_busy_s = 0
        p.wait("sync", "c3", 1)
        ykng = p.dma("sync", kn2g[0:1, :], kng_d, sem="kn")

        def load_kv(h):
            buf = h % 2
            p.wait("sync", "t", kv_used_t[buf])
            p.wait("sync", "c1", 1)
            p.wait("sync", "c2", 1)
            for r in range(N_CORES):
                ys = p.dma("sync", kbuf[:, buf * 8192 + r * 1024: buf * 8192 + (r + 1) * 1024],
                           k_g[r * D + h * 128: r * D + h * 128 + 128, :], sem=f"kv{buf}")
            for r in range(N_CORES):
                src = v_g[r * 1024:(r + 1) * 1024, h * 128:(h + 1) * 128]
                src3 = src.rearrange("(a q) d -> q a d", q=128)
                dst3 = vbuf[:, buf * 8192 + r * 1024: buf * 8192 + (r + 1) * 1024].rearrange(
                    "q (a d) -> q a d", d=128)
                ys = p.dma("sync", dst3, src3, sem=f"kv{buf}")
            kv_load[buf] = ys

        for h in range(HEADS):
            buf = h % 2
            if h == 0:
                load_kv(0)
                load_kv(1)
            elif h < HEADS - 1:
                load_kv(h + 1)
            p.waitp("tensor", kv_load[buf])
            for qw in range(2):
                slot = qw
                p.wait("sync", "v", qb_free[slot])
                yq = p.dma("sync", qb[:, slot * 512:(slot + 1) * 512],
                           q_spill[h * 128:(h + 1) * 128, qw * 512:(qw + 1) * 512],
                           sem=f"qb{slot}")
                p.waitp("vector", ykng)
                vchain(lambda e, h=h, qw=qw: e.tensor_tensor(
                    scol(C_C2), qn2[0:1, h * 2 + qw: h * 2 + qw + 1],
                    kn2g[0:1, h:h + 1], ALU.mult))
                vv = vflush()
                p.wait("scalar", "v", vv)
                schain(lambda e: e.sqrt(scols[0:1, S_C:S_C + 1], scol(C_C2)))
                sv = sflush()
                p.wait("vector", "s", sv)
                vchain(lambda e: e.tensor_scalar(
                    scol(C_BV), scols[0:1, S_C:S_C + 1], -SCALE, 12.0,
                    ALU.mult, ALU.add))
                vv = vflush()
                p.wait("tensor", "v", vv)
                p.wait("tensor", "v", pd_guard_v)
                tb = p.op("tensor", lambda e: e.matmul(
                    pd[:, 0:1], onesr_f[0:1, :], scol(C_BV), start=True, stop=True), inc="t")
                p.wait("scalar", "t", tb)
                sb_ = p.op("scalar", lambda e, slot=slot: e.copy(
                    bias_h[:, slot:slot + 1], pd[:, 0:1]), inc="s")
                pd_busy_s = sb_

                p.waitp("tensor", yq)
                vl = p.op("vector", lambda e: e.memset(lpp[:, 0:512], 0.0), inc="v")

                pcs = pc[:, (unit % 2) * 512:(unit % 2 + 1) * 512]

                def emit_pv(pq, first_wait):
                    parq = pq % 2
                    p.wait("tensor", "s", last_exp[parq])
                    if first_wait:
                        p.wait("tensor", "v", pc_free[unit % 2])
                    for jj in range(2):
                        kt = 2 * pq + jj
                        tpv = p.op("tensor", lambda e, jj=jj, kt=kt, buf=buf, parq=parq, pq=pq, pcs=pcs: e.matmul(
                            pcs, vbuf[:, buf * 8192 + kt * 128: buf * 8192 + kt * 128 + 128],
                            prb[:, parq * 1024 + jj * 512: parq * 1024 + (jj + 1) * 512],
                            start=(pq == 0 and jj == 0), stop=(pq == 31 and jj == 1),
                            skip_group_check=True), inc="t")
                    last_pv[parq] = tpv

                for pp in range(32):
                    par = pp % 2
                    pair = pa if par == 0 else pb
                    p.wait("tensor", "s", last_exp[par])
                    for jj in range(2):
                        kt = 2 * pp + jj
                        tsc = p.op("tensor", lambda e, pr=pair, jj=jj, kt=kt, buf=buf, slot=slot: e.matmul(
                            pr[:, jj * 512:(jj + 1) * 512],
                            kbuf[:, buf * 8192 + kt * 128: buf * 8192 + kt * 128 + 128],
                            qb[:, slot * 512:(slot + 1) * 512],
                            start=True, stop=True), inc="t")
                    p.wait("scalar", "t", tsc)
                    p.wait("scalar", "t", last_pv[par])
                    p.wait("scalar", "v", last_ps[par])
                    se = p.op("scalar", lambda e, pr=pair, par=par, slot=slot: e.activation(
                        prb[:, par * 1024:(par + 1) * 1024], pr[:],
                        AF.Exp, bias=bias_h[:, slot:slot + 1], scale=SCALE), inc="s")
                    last_exp[par] = se
                    if pp >= 1:
                        emit_pv(pp - 1, pp == 1)
                    p.wait("vector", "s", se)
                    p.op("vector", lambda e, par=par: e.tensor_tensor(
                        pssum[:], prb[:, par * 1024:par * 1024 + 512],
                        prb[:, par * 1024 + 512:(par + 1) * 1024], ALU.add))
                    vl = p.op("vector", lambda e, pp=pp: e.tensor_tensor(
                        lpp[:, ((pp + 1) % 2) * 512:((pp + 1) % 2 + 1) * 512],
                        lpp[:, (pp % 2) * 512:(pp % 2) * 512 + 512],
                        pssum[:], ALU.add), inc="v")
                    last_ps[par] = vl
                emit_pv(31, False)
                kv_used_t[buf] = last_pv[1]
                p.wait("tensor", "v", vl)
                p.wait("tensor", "s", pd_busy_s)
                tl = p.op("tensor", lambda e: e.matmul(
                    pd[0:1, 0:512], ones_f[:], lpp[:, 0:512], start=True, stop=True), inc="t")
                p.wait("vector", "t", tl)
                vr = p.op("vector", lambda e: e.reciprocal(rcp[:], pd[0:1, 0:512]), inc="v")
                p.wait("tensor", "v", vr)
                tbc = p.op("tensor", lambda e: e.matmul(
                    pd[:], onesr_f[0:1, :], rcp[0:1, :], start=True, stop=True), inc="t")
                p.wait("vector", "t", tbc)
                p.op("vector", lambda e: e.tensor_copy(rbb[:], pd[:]))
                p.wait("vector", "t", last_pv[1])
                p.waitp("vector", ctxs_free[slot])
                vctx = p.op("vector", lambda e, slot=slot, pcs=pcs: e.tensor_tensor(
                    ctxs[:, slot * 512:(slot + 1) * 512], pcs, rbb[:], ALU.mult), inc="v")
                pc_free[unit % 2] = vctx
                pd_guard_v = vctx
                qb_free[slot] = vctx
                unit += 1
                p.wait("sync", "v", vctx)
                ctxs_free[slot] = p.dma(
                    "sync", ctx_dram[h * 128:(h + 1) * 128, qw * 512:(qw + 1) * 512],
                    ctxs[:, slot * 512:(slot + 1) * 512], sem=f"cx{slot}")
    p.barrier()

    with (
        nc.sbuf_tensor("ctx_fm", [128, 32 * 1024], BF16) as ctx_fm,
        nc.sbuf_tensor("owbuf", [128, 2 * 32 * 512], BF16) as owbuf,
        nc.sbuf_tensor("ob_bc", [128, 4096], F32) as ob_bc,
        nc.sbuf_tensor("xres", [128, 2 * 512], F32) as xres,
        nc.sbuf_tensor("stga", [128, 2 * 512], F32) as stga,
        nc.sbuf_tensor("stgb", [128, 2 * 512], F32) as stgb,
        nc.psum_tensor("po", [128, 1024], F32) as po,
        nc.psum_tensor("pi4", [128, 512], F32) as pi4,
    ):
        t_ob = rowb_free_t
        s_ob = 0
        for gidx in range(8):
            slotr = gidx % 2
            rb = rowb[0:1, slotr * 512:(slotr + 1) * 512]
            p.wait("sync", "t", t_ob)
            yr = p.dma("sync", rb, ob[gidx * 512:(gidx + 1) * 512].rearrange("(a b) -> a b", a=1),
                       sem=f"rb{slotr}")
            p.waitp("tensor", yr)
            p.wait("tensor", "s", s_ob)
            t_ob = p.op("tensor", lambda e, rb=rb: e.matmul(
                pi4[:], onesr_f[0:1, :], rb, start=True, stop=True), inc="t")
            p.wait("scalar", "t", t_ob)
            s_ob = p.op("scalar", lambda e, gidx=gidx: e.copy(
                ob_bc[:, gidx * 512:(gidx + 1) * 512], pi4[:]), inc="s")
        rowb_free_t = t_ob
        p.wait("sync", "cx0", p.cnt.get("cx0", 0))
        p.wait("sync", "cx1", p.cnt.get("cx1", 0))
        ctx3 = ctx_dram.rearrange("(c p) t -> p c t", p=128)
        yctx = p.dma("sync", ctx_fm[:].rearrange("p (c t) -> p c t", t=1024), ctx3, sem="big")
        ow3 = ow.rearrange("(c p) o -> p c o", p=128)
        ow_done_t = [0, 0]
        ow_load = [None, None]
        po_free = [0, 0]
        xres_y = [None, None]
        xres_free = [0, 0]
        x1st_free = [None, None]
        for ofg in range(8):
            pbuf = ofg % 2
            p.wait("gpsimd", "t", ow_done_t[pbuf])
            wdst = owbuf[:, pbuf * 16384:(pbuf + 1) * 16384].rearrange("p (c o) -> p c o", o=512)
            ow_load[pbuf] = p.dma("gpsimd", wdst, ow3[:, :, ofg * 512:(ofg + 1) * 512],
                                  sem=f"w{pbuf}")
            for tt in range(8):
                pslot = po[:, (tt % 2) * 512:(tt % 2 + 1) * 512]
                p.waitp("tensor", ow_load[pbuf])
                p.waitp("tensor", yctx)
                p.wait("tensor", "v", po_free[tt % 2])
                p.wait("tensor", "s", s_ob)
                for cf in range(32):
                    tof = p.op("tensor", lambda e, ps=pslot, cf=cf, tt=tt, pbuf=pbuf: e.matmul(
                        ps, ctx_fm[:, cf * 1024 + tt * 128: cf * 1024 + tt * 128 + 128],
                        owbuf[:, pbuf * 16384 + cf * 512:pbuf * 16384 + (cf + 1) * 512],
                        start=(cf == 0), stop=(cf == 31)), inc="t")
                ow_done_t[pbuf] = tof
                xslot = tt % 2
                p.wait("sync", "v", xres_free[xslot])
                xres_y[xslot] = p.dma("sync", xres[:, xslot * 512:(xslot + 1) * 512],
                                      xc[tt * 128:(tt + 1) * 128, ofg * 512:(ofg + 1) * 512],
                                      sem=f"xr{xslot}")
                p.wait("vector", "t", tof)
                p.waitp("vector", xres_y[xslot])
                p.waitp("vector", x1st_free[tt % 2])
                p.op("vector", lambda e, ps=pslot, tt=tt, ofg=ofg: e.tensor_tensor(
                    stga[:, (tt % 2) * 512:(tt % 2 + 1) * 512], ps,
                    ob_bc[:, ofg * 512:(ofg + 1) * 512], ALU.add))
                vev = p.op("vector", lambda e, tt=tt, xslot=xslot: e.tensor_tensor(
                    stgb[:, (tt % 2) * 512:(tt % 2 + 1) * 512],
                    stga[:, (tt % 2) * 512:(tt % 2 + 1) * 512],
                    xres[:, xslot * 512:(xslot + 1) * 512], ALU.add), inc="v")
                po_free[tt % 2] = vev
                xres_free[xslot] = vev
                p.wait("sync", "v", vev)
                x1st_free[tt % 2] = p.dma(
                    "sync", x1_dram[tt * 128:(tt + 1) * 128, ofg * 512:(ofg + 1) * 512],
                    stgb[:, (tt % 2) * 512:(tt % 2 + 1) * 512], sem=f"xs{tt % 2}")
    p.barrier()

    with (
        nc.sbuf_tensor("h1_fm", [128, 32 * 1024], BF16) as h1_fm,
        nc.sbuf_tensor("x1t", [128, 4096], F32) as x1t,
        nc.sbuf_tensor("h1t", [128, 4096], F32) as h1t,
        nc.sbuf_tensor("h1bf", [128, 4096], BF16) as h1bf,
        nc.sbuf_tensor("sq4", [128, 4096], F32) as sq4,
        nc.psum_tensor("pt4", [128, 2 * 512], F32) as pt4,
    ):
        x1_cons_s = 0
        x1_cons_v = 0
        sq4_cons_v = 0
        h1bf_t = 0
        last_tr_s = 0
        h1t_st = None
        for tt in range(8):
            p.wait("sync", "s", x1_cons_s)
            p.wait("sync", "v", x1_cons_v)
            yx1 = p.dma("sync", x1t[:], x1_dram[tt * 128:(tt + 1) * 128, :], sem="xl0")
            p.waitp("scalar", yx1)
            p.wait("scalar", "v", sq4_cons_v)
            schain(lambda e: e.activation(sq4[:], x1t[:], AF.Square,
                                          accum_out=sco(S_SQ)))
            ssq = sflush()
            x1_cons_s = ssq
            p.waitp("vector", yx1)
            vchain(lambda e: e.tensor_reduce(col(C_SUM), x1t[:], AX.X, ALU.add))
            p.wait("vector", "s", ssq)
            sq4_cons_v = ln_stats(sco(S_SQ))
            p.waitp("vector", h1t_st)
            vap = p.op("vector", lambda e: e.tensor_scalar(
                h1t[:], x1t[:], col(C_MU), col(C_RSTD), ALU.subtract, ALU.mult), inc="v")
            p.wait("vector", "t", h1bf_t)
            v_apply = p.op("vector", lambda e: e.tensor_copy(h1bf[:], h1t[:]), inc="v")
            x1_cons_v = v_apply
            p.wait("sync", "v", v_apply)
            h1t_st = p.dma("sync", h1_dram[tt * 128:(tt + 1) * 128, :], h1t[:], sem="hs")
            p.wait("tensor", "v", v_apply)
            for fcg in range(8):
                ptsl = pt4[:, (fcg % 2) * 512:(fcg % 2 + 1) * 512]
                p.wait("tensor", "s", last_tr_s)
                for j in range(4):
                    fc = fcg * 4 + j
                    h1bf_t = p.op("tensor", lambda e, ps=ptsl, fc=fc, j=j: e.matmul(
                        ps[:, j * 128:(j + 1) * 128], h1bf[:, fc * 128:(fc + 1) * 128],
                        i_bf[:], start=True, stop=True), inc="t")
                p.wait("scalar", "t", h1bf_t)
                dst = h1_fm[:].rearrange("q (c t) -> q c t", t=1024)[
                    :, fcg * 4:(fcg + 1) * 4, tt * 128:(tt + 1) * 128]
                src = ptsl.rearrange("q (c t) -> q c t", t=128)
                last_tr_s = p.op("scalar", lambda e, d=dst, sp=src: e.copy(d, sp), inc="s")
        p.wait("sync", "s", last_tr_s)
        for fc in range(32):
            p.dma("sync", h1fm_dram[fc * 128:(fc + 1) * 128, :], h1_fm[:, fc * 1024:(fc + 1) * 1024])
        h1fm_y = p.cnt["y"]
    p.barrier()

    up3 = up_w.rearrange("(c p) o -> p c o", p=128)
    down3 = down_w.rearrange("(c p) o -> p c o", p=128)
    fc13 = fc1_w.rearrange("(c p) o -> p c o", p=128)
    fc23 = fc2_w.rearrange("(c p) o -> p c o", p=128)
    ex3 = expert_w.rearrange("(c p) o -> p c o", p=128)
    h1fm3 = h1fm_dram.rearrange("(c p) t -> p c t", p=128)

    with (
        nc.sbuf_tensor("hfm", [128, 64 * 512], BF16) as hfm,
        nc.sbuf_tensor("wb5", [128, 2 * 8192], BF16) as wb5,
        nc.sbuf_tensor("efm", [128, 16 * 512], BF16) as efm,
        nc.sbuf_tensor("g5", [8, 512], F32) as g5,
        nc.sbuf_tensor("rl5", [128, 512], F32) as rl5,
        nc.sbuf_tensor("ssc", [128, 512], F32) as ssc,
        nc.sbuf_tensor("ost", [128, 2 * 512], F32) as ost,
        nc.psum_tensor("pu", [128, 1024], F32) as pu,
        nc.psum_tensor("pu2", [128, 1024], F32) as pu2,
        nc.psum_tensor("pf2", [128, 512], F32) as pf2,
        nc.psum_tensor("pg5", [128, 512], F32) as pg5,
    ):
        p_ttl = [lambda: pu[:, 0:512], lambda: pu[:, 512:1024],
                 lambda: pu2[:, 0:512], lambda: pu2[:, 512:1024]]
        wb5_free_t = [0, 0]
        fc2b_free_t = [0, 0]
        dnx_free_t = [0, 0]
        pslot_free_s = [0, 0, 0, 0]
        pf2_free_v = 0
        ssc_free_v = 0
        ost_free = [None, None]
        g5_cons_t = 0
        scols_cons_carry = 0
        epi_done_v = 0
        hfm_cons_t = 0
        efm_cons_t = 0
        h1w_cons_t = 0
        for w in range(2):
            with nc.sbuf_tensor(f"h1w{w}", [128, 32 * 512], BF16) as h1w:
                p.wait("sync", "y", h1fm_y)
                p.wait("sync", "t", h1w_cons_t)
                p.wait("sync", "v", epi_done_v)
                yh1w = p.dma("sync", h1w[:].rearrange("p (c t) -> p c t", t=512),
                             h1fm3[:, :, w * 512:(w + 1) * 512], sem="big")
                slotpu = 0
                wl = None
                pb5 = 0
                for lc in range(16):
                    if lc % 2 == 0:
                        pb5 = (lc // 2) % 2
                        p.wait("gpsimd", "t", wb5_free_t[pb5])
                        wdst = wb5[:, pb5 * 8192:(pb5 + 1) * 8192].rearrange("p (c o) -> p c o", o=256)
                        wl = p.dma("gpsimd", wdst, fc13[:, :, (lc // 2) * 256:(lc // 2) * 256 + 256],
                                   sem=f"w{pb5}")
                    sl = slotpu % 4
                    ps = p_ttl[sl]()
                    p.waitp("tensor", wl)
                    p.waitp("tensor", yh1w)
                    p.wait("tensor", "s", pslot_free_s[sl])
                    for fc in range(32):
                        lt = wb5[:, pb5 * 8192 + fc * 256 + (lc % 2) * 128:
                                 pb5 * 8192 + fc * 256 + (lc % 2) * 128 + 128]
                        tfc1 = p.op("tensor", lambda e, ps=ps, lt=lt, fc=fc: e.matmul(
                            ps, lt, h1w[:, fc * 512:(fc + 1) * 512],
                            start=(fc == 0), stop=(fc == 31)), inc="t")
                    wb5_free_t[pb5] = tfc1
                    p.wait("scalar", "t", tfc1)
                    p.wait("scalar", "t", hfm_cons_t)
                    pslot_free_s[sl] = p.op("scalar", lambda e, lc=lc, ps=ps: e.copy(
                        hfm[:, lc * 512:(lc + 1) * 512], ps), inc="s")
                    slotpu += 1
                lat_ready_s = max(pslot_free_s)
                for mc in range(16):
                    if mc % 2 == 0:
                        pb5 = (mc // 2) % 2
                        p.wait("gpsimd", "t", wb5_free_t[pb5])
                        wdst = wb5[:, pb5 * 8192:pb5 * 8192 + 4096].rearrange("p (c o) -> p c o", o=256)
                        wl = p.dma("gpsimd", wdst, ex3[:, :, (mc // 2) * 256:(mc // 2) * 256 + 256],
                                   sem=f"w{pb5}")
                    sl = slotpu % 4
                    ps = p_ttl[sl]()
                    p.waitp("tensor", wl)
                    p.wait("tensor", "s", lat_ready_s)
                    p.wait("tensor", "s", pslot_free_s[sl])
                    for lc in range(16):
                        lt = wb5[:, pb5 * 8192 + lc * 256 + (mc % 2) * 128:
                                 pb5 * 8192 + lc * 256 + (mc % 2) * 128 + 128]
                        tex = p.op("tensor", lambda e, ps=ps, lt=lt, lc=lc: e.matmul(
                            ps, lt, hfm[:, lc * 512:(lc + 1) * 512],
                            start=(lc == 0), stop=(lc == 15)), inc="t")
                    wb5_free_t[pb5] = tex
                    p.wait("scalar", "t", tex)
                    p.wait("scalar", "t", efm_cons_t)
                    pslot_free_s[sl] = p.op("scalar", lambda e, mc=mc, ps=ps: e.copy(
                        efm[:, mc * 512:(mc + 1) * 512], ps), inc="s")
                    slotpu += 1
                efm_ready_s = max(pslot_free_s)
                lat_last_read_t = wb5_free_t[pb5]
                p.waitp("tensor", yh1w)
                p.wait("tensor", "big", big_init)
                p.wait("tensor", "s", g5_cons_t * 0)
                for fc in range(32):
                    tg = p.op("tensor", lambda e, fc=fc: e.matmul(
                        pg5[0:8, :], gate_sb[:, fc * 8:(fc + 1) * 8], h1w[:, fc * 512:(fc + 1) * 512],
                        start=(fc == 0), stop=(fc == 31)), inc="t")
                p.wait("scalar", "t", tg)
                p.op("scalar", lambda e: e.copy(g5[:], pg5[0:8, :]))
                sg5 = sflush()
                vgs = 0
                for ttl in range(4):
                    p.wait("tensor", "s", sg5)
                    p.wait("tensor", "v", vgs)
                    p.wait("tensor", "v", pf2_free_v)
                    tpp = p.op("tensor", lambda e, ttl=ttl: e.matmul(
                        pf2[:, 0:28], g5[0:8, ttl * 128:(ttl + 1) * 128], p_sb[:],
                        start=True, stop=True), inc="t")
                    p.wait("vector", "t", tpp)
                    vgs = p.op("vector", lambda e, w=w, ttl=ttl: e.tensor_reduce(
                        gsum_cols[:, w * 4 + ttl: w * 4 + ttl + 1], pf2[:, 0:28],
                        AX.X, ALU.max), inc="v")
                pf2_free_v = vgs
                gs_ready_v = vgs
                g5_cons_t = tpp
                rl_free_s = 0
                for ic in range(64):
                    if ic % 2 == 0:
                        pb5 = (ic // 2) % 2
                        p.wait("gpsimd", "t", wb5_free_t[pb5])
                        wdst = wb5[:, pb5 * 8192:(pb5 + 1) * 8192].rearrange("p (c o) -> p c o", o=256)
                        wl = p.dma("gpsimd", wdst, up3[:, :, (ic // 2) * 256:(ic // 2) * 256 + 256],
                                   sem=f"w{pb5}")
                    sl = slotpu % 4
                    ps = p_ttl[sl]()
                    p.waitp("tensor", wl)
                    p.wait("tensor", "s", pslot_free_s[sl])
                    for fc in range(32):
                        lt = wb5[:, pb5 * 8192 + fc * 256 + (ic % 2) * 128:
                                 pb5 * 8192 + fc * 256 + (ic % 2) * 128 + 128]
                        tup = p.op("tensor", lambda e, ps=ps, lt=lt, fc=fc: e.matmul(
                            ps, lt, h1w[:, fc * 512:(fc + 1) * 512],
                            start=(fc == 0), stop=(fc == 31)), inc="t")
                    wb5_free_t[pb5] = tup
                    p.wait("scalar", "t", tup)
                    p.wait("scalar", "s", rl_free_s)
                    if ic < 16:
                        p.wait("scalar", "t", lat_last_read_t)
                    p.op("scalar", lambda e, ps=ps: e.activation(rl5[:], ps, AF.Relu))
                    pslot_free_s[sl] = p.op("scalar", lambda e, ic=ic: e.activation(
                        hfm[:, ic * 512:(ic + 1) * 512], rl5[:], AF.Square), inc="s")
                    rl_free_s = pslot_free_s[sl]
                    slotpu += 1
                hfm_ready_s = max(pslot_free_s)
                h1w_cons_t = max(wb5_free_t + [tg])
            with (
                nc.sbuf_tensor(f"downx{w}", [128, 2 * 8192], BF16) as dnx,
                nc.sbuf_tensor(f"fc2b{w}", [128, 2 * 8192], BF16) as fc2b,
            ):
                pttl_stop_t = [0, 0, 0, 0]
                for hfg in range(8):
                    fcb = hfg % 2
                    p.wait("gpsimd", "t", fc2b_free_t[fcb])
                    wdst = fc2b[:, fcb * 8192:(fcb + 1) * 8192].rearrange("p (c o) -> p c o", o=512)
                    yfc2 = p.dma("gpsimd", wdst, fc23[:, :, hfg * 512:(hfg + 1) * 512],
                                 sem=f"f2{fcb}")
                    qslots = [wb5[:, 0:8192], wb5[:, 8192:16384], dnx[:, 0:8192], dnx[:, 8192:16384]]
                    yq4 = [None] * 4
                    for iq in range(4):
                        if iq < 2:
                            p.wait("gpsimd", "t", wb5_free_t[iq])
                        else:
                            p.wait("gpsimd", "t", dnx_free_t[iq - 2])
                            p.wait("gpsimd", "t", h1w_cons_t)
                        wdst2 = qslots[iq].rearrange("p (c o) -> p c o", o=512)
                        semn = f"w{iq}" if iq < 2 else f"dn{iq - 2}"
                        yq4[iq] = p.dma("gpsimd", wdst2,
                                        down3[:, iq * 16:(iq + 1) * 16, hfg * 512:(hfg + 1) * 512],
                                        sem=semn)
                    tdn = 0
                    for iq in range(4):
                        p.waitp("tensor", yq4[iq])
                        if iq == 0:
                            p.wait("tensor", "s", hfm_ready_s)
                            for ttl in range(4):
                                p.wait("tensor", "s", pslot_free_s[ttl])
                        for ic in range(16):
                            i_ch = iq * 16 + ic
                            for ttl in range(4):
                                tdn = p.op("tensor", lambda e, iq=iq, ic=ic, ttl=ttl, i_ch=i_ch: e.matmul(
                                    p_ttl[ttl](),
                                    hfm[:, i_ch * 512 + ttl * 128: i_ch * 512 + ttl * 128 + 128],
                                    qslots[iq][:, ic * 512:(ic + 1) * 512],
                                    start=(i_ch == 0), stop=(i_ch == 63),
                                    skip_group_check=True), inc="t")
                                if i_ch == 63:
                                    pttl_stop_t[ttl] = tdn
                        if iq < 2:
                            wb5_free_t[iq] = tdn
                        else:
                            dnx_free_t[iq - 2] = tdn
                    hfm_cons_t = tdn
                    for ttl in range(4):
                        p.waitp("tensor", yfc2)
                        p.wait("tensor", "s", efm_ready_s)
                        p.wait("tensor", "v", pf2_free_v)
                        for mc in range(16):
                            tf2 = p.op("tensor", lambda e, mc=mc, ttl=ttl, fcb=fcb: e.matmul(
                                pf2[:], efm[:, mc * 512 + ttl * 128: mc * 512 + ttl * 128 + 128],
                                fc2b[:, fcb * 8192 + mc * 512: fcb * 8192 + (mc + 1) * 512],
                                start=(mc == 0), stop=(mc == 15)), inc="t")
                        fc2b_free_t[fcb] = tf2
                        efm_cons_t = tf2
                        p.wait("scalar", "t", pttl_stop_t[ttl])
                        p.wait("scalar", "v", ssc_free_v)
                        sdc = p.op("scalar", lambda e, ttl=ttl: e.copy(ssc[:], p_ttl[ttl]()), inc="s")
                        pslot_free_s[ttl] = sdc
                        oslot = (hfg * 4 + ttl) % 2
                        p.wait("vector", "t", tf2)
                        p.wait("vector", "s", sdc)
                        p.wait("vector", "v", gs_ready_v)
                        p.waitp("vector", ost_free[oslot])
                        vpre = p.op("vector", lambda e, w=w, ttl=ttl, oslot=oslot: e.scalar_tensor_tensor(
                            ost[:, oslot * 512:(oslot + 1) * 512], pf2[:],
                            gsum_cols[:, w * 4 + ttl: w * 4 + ttl + 1], ssc[:],
                            ALU.mult, ALU.add), inc="v")
                        pf2_free_v = vpre
                        ssc_free_v = vpre
                        p.wait("sync", "v", vpre)
                        ost_free[oslot] = p.dma(
                            "sync",
                            pre_dram[w * 512 + ttl * 128: w * 512 + (ttl + 1) * 128,
                                     hfg * 512:(hfg + 1) * 512],
                            ost[:, oslot * 512:(oslot + 1) * 512], sem=f"os{oslot}")
            with contextlib.ExitStack() as _epi:
                gbc = _epi.enter_context(nc.sbuf_tensor(f"gbc{w}", [128, 4096], F32))
                bbc = _epi.enter_context(nc.sbuf_tensor(f"bbc{w}", [128, 4096], F32))
                rch = _epi.enter_context(nc.sbuf_tensor(f"rch{w}", [128, 2 * 512], F32))
                sch = _epi.enter_context(nc.sbuf_tensor(f"sch{w}", [128, 2 * 512], F32))
                h1r = _epi.enter_context(nc.sbuf_tensor(f"h1r{w}", [128, 2 * 512], F32))
                z_sb = _epi.enter_context(nc.sbuf_tensor(f"zsb{w}", [128, 4096], F32))
                pe5 = _epi.enter_context(nc.psum_tensor(f"pe5{w}", [128, 512], F32))
                s_bc = 0
                t_bc = rowb_free_t
                for which, (src_ap, dstt) in enumerate(((ln_g, gbc), (ln_b, bbc))):
                    for gidx in range(8):
                        slotr = gidx % 2
                        rb = rowb[0:1, slotr * 512:(slotr + 1) * 512]
                        p.wait("sync", "t", t_bc)
                        yr = p.dma("sync", rb,
                                   src_ap[gidx * 512:(gidx + 1) * 512].rearrange("(a b) -> a b", a=1),
                                   sem=f"rb{slotr}")
                        p.waitp("tensor", yr)
                        p.wait("tensor", "s", s_bc)
                        p.wait("tensor", "t", hfm_cons_t)
                        t_bc = p.op("tensor", lambda e, rb=rb: e.matmul(
                            pe5[:], onesr_f[0:1, :], rb, start=True, stop=True), inc="t")
                        p.wait("scalar", "t", t_bc)
                        s_bc = p.op("scalar", lambda e, d=dstt, gidx=gidx: e.copy(
                            d[:, gidx * 512:(gidx + 1) * 512], pe5[:]), inc="s")
                rowb_free_t = t_bc
                bc_ready = s_bc
                last_v = 0
                scols_cons = scols_cons_carry
                for ttl in range(4):
                    trow = w * 512 + ttl * 128
                    p.wait("sync", "os0", p.cnt.get("os0", 0))
                    p.wait("sync", "os1", p.cnt.get("os1", 0))
                    rload = [None, None]
                    rfree = [0, 0]
                    ssq_all = 0
                    for c in range(8):
                        sl = c % 2
                        p.wait("sync", "v", rfree[sl])
                        rload[sl] = p.dma("sync", rch[:, sl * 512:(sl + 1) * 512],
                                          pre_dram[trow:trow + 128, c * 512:(c + 1) * 512],
                                          sem=f"rc{sl}")
                        p.waitp("scalar", rload[sl])
                        p.wait("scalar", "v", scols_cons)
                        fn = lambda e, sl=sl, c=c: e.activation(
                            rl5[:], rch[:, sl * 512:(sl + 1) * 512], AF.Square,
                            accum_out=scols[:, c:c + 1])
                        if c == 7:
                            schain(fn)
                            ssq_all = sflush()
                        else:
                            p.op("scalar", fn)
                        p.waitp("vector", rload[sl])
                        rfree[sl] = p.op("vector", lambda e, sl=sl, c=c: e.tensor_reduce(
                            cols[:, 16 + c:17 + c], rch[:, sl * 512:(sl + 1) * 512],
                            AX.X, ALU.add), inc="v")
                    p.wait("vector", "v", rfree[1])
                    vchain(lambda e: e.tensor_reduce(col(C_SUM), cols[:, 16:24], AX.X, ALU.add))
                    p.wait("vector", "s", ssq_all)
                    vchain(lambda e: e.tensor_reduce(col(C_SQ), scols[:, 0:8], AX.X, ALU.add))
                    scols_cons = ln_stats(col(C_SQ))
                    hfree = [0, 0]
                    ssq_all = 0
                    for c in range(8):
                        sl = c % 2
                        p.wait("sync", "v", rfree[sl])
                        rload[sl] = p.dma("sync", rch[:, sl * 512:(sl + 1) * 512],
                                          pre_dram[trow:trow + 128, c * 512:(c + 1) * 512],
                                          sem=f"rc{sl}")
                        p.wait("sync", "v", hfree[sl])
                        yh = p.dma("sync", h1r[:, sl * 512:(sl + 1) * 512],
                                   h1_dram[trow:trow + 128, c * 512:(c + 1) * 512],
                                   sem=f"hr{sl}")
                        p.waitp("vector", yh)
                        p.waitp("vector", rload[sl])
                        p.wait("vector", "s", bc_ready)
                        p.op("vector", lambda e, sl=sl: e.tensor_scalar(
                            sch[:, 0:512], rch[:, sl * 512:(sl + 1) * 512],
                            col(C_MU), col(C_RSTD), ALU.subtract, ALU.mult))
                        p.op("vector", lambda e, c=c: e.tensor_tensor(
                            sch[:, 512:1024], sch[:, 0:512], gbc[:, c * 512:(c + 1) * 512], ALU.mult))
                        p.op("vector", lambda e, c=c: e.tensor_tensor(
                            sch[:, 0:512], sch[:, 512:1024], bbc[:, c * 512:(c + 1) * 512], ALU.add))
                        vz = vchain(lambda e, sl=sl, c=c: e.tensor_tensor(
                            z_sb[:, c * 512:(c + 1) * 512], sch[:, 0:512],
                            h1r[:, sl * 512:(sl + 1) * 512], ALU.add))
                        hfree[sl] = vz
                        vzr = p.op("vector", lambda e, c=c: e.tensor_reduce(
                            cols[:, 16 + c:17 + c], z_sb[:, c * 512:(c + 1) * 512],
                            AX.X, ALU.add), inc="v")
                        rfree[sl] = vzr
                        p.wait("scalar", "v", vz)
                        fn = lambda e, c=c: e.activation(
                            rl5[:], z_sb[:, c * 512:(c + 1) * 512], AF.Square,
                            accum_out=scols[:, c:c + 1])
                        if c == 7:
                            schain(fn)
                            ssq_all = sflush()
                        else:
                            p.op("scalar", fn)
                    p.wait("vector", "v", rfree[1])
                    vchain(lambda e: e.tensor_reduce(col(C_SUM), cols[:, 16:24], AX.X, ALU.add))
                    p.wait("vector", "s", ssq_all)
                    vchain(lambda e: e.tensor_reduce(col(C_SQ), scols[:, 0:8], AX.X, ALU.add))
                    scols_cons = ln_stats(col(C_SQ))
                    ofree = [None, None]
                    for c in range(8):
                        sl = c % 2
                        p.waitp("vector", ofree[sl])
                        vo = p.op("vector", lambda e, sl=sl, c=c: e.tensor_scalar(
                            ost[:, sl * 512:(sl + 1) * 512], z_sb[:, c * 512:(c + 1) * 512],
                            col(C_MU), col(C_RSTD), ALU.subtract, ALU.mult), inc="v")
                        last_v = vo
                        p.wait("sync", "v", vo)
                        ofree[sl] = p.dma("sync", out_d[trow:trow + 128, c * 512:(c + 1) * 512],
                                          ost[:, sl * 512:(sl + 1) * 512], sem=f"ot{sl}")
                epi_done_v = last_v
                scols_cons_carry = scols_cons
            p.barrier()

    if DBG:
        for sname in list(p.cnt.keys()):
            p.wait("sync", sname, p.cnt[sname])
        p.dma("sync", x1_dbg, x1_dram)
        p.dma("sync", h1_dbg, h1_dram)
        p.dma("sync", ctx_dbg, ctx_dram)
        p.dma("sync", kin_dbg, k_in)
        p.dma("sync", vin_dbg, v_in)
        p.dma("sync", qsp_dbg, q_spill)
        p.dma("sync", nrm_dbg[0:1, 0:64], qn2[0:1, :])
        p.dma("sync", nrm_dbg[0:1, 64:96], kn2l[0:1, :])
        p.dma("sync", nrm_dbg[0:1, 96:128], kn2g[0:1, :])
    for sname in list(p.cnt.keys()):
        p.wait("sync", sname, p.cnt[sname])

    p.finalize_check()

    with nc.Block() as block:
        @block.tensor
        def _(e):
            for f in p.ops["tensor"]:
                f(e)

        @block.scalar
        def _(e):
            for f in p.ops["scalar"]:
                f(e)

        @block.vector
        def _(e):
            for f in p.ops["vector"]:
                f(e)

        @block.gpsimd
        def _(e):
            for f in p.ops["gpsimd"]:
                f(e)

        @block.sync
        def _(e):
            for f in p.ops["sync"]:
                f(e)
    return nc


def kernel(L0_attn_x, L0_attn_attn_qkvw, L0_attn_attn_qkvb, L0_attn_attn_ow,
           L0_attn_attn_ob, L0_ffn_gate_w, L0_ffn_up_w, L0_ffn_down_w,
           L0_ffn_fc1_w, L0_ffn_fc2_w, L0_ffn_expert_w, L0_ffn_ln_gamma,
           L0_ffn_ln_beta):
    x = np.ascontiguousarray(np.asarray(L0_attn_x, dtype=np.float32).reshape(S, D))
    weights = {
        "qkvw": np.ascontiguousarray(np.asarray(L0_attn_attn_qkvw, np.float32)),
        "qkvb": np.ascontiguousarray(np.asarray(L0_attn_attn_qkvb, np.float32)),
        "ow": np.ascontiguousarray(np.asarray(L0_attn_attn_ow, np.float32)),
        "ob": np.ascontiguousarray(np.asarray(L0_attn_attn_ob, np.float32)),
        "gate_w": np.ascontiguousarray(np.asarray(L0_ffn_gate_w, np.float32)),
        "up_w": np.ascontiguousarray(np.asarray(L0_ffn_up_w, np.float32)),
        "down_w": np.ascontiguousarray(np.asarray(L0_ffn_down_w, np.float32)),
        "fc1_w": np.ascontiguousarray(np.asarray(L0_ffn_fc1_w, np.float32)),
        "fc2_w": np.ascontiguousarray(np.asarray(L0_ffn_fc2_w, np.float32)),
        "expert_w": np.ascontiguousarray(np.asarray(L0_ffn_expert_w, np.float32)),
        "ln_g": np.ascontiguousarray(np.asarray(L0_ffn_ln_gamma, np.float32)),
        "ln_b": np.ascontiguousarray(np.asarray(L0_ffn_ln_beta, np.float32)),
    }
    nc = bass.Bass("TRN2", target_bir_lowering=False, num_devices=N_CORES)
    build(nc, DBG=DBG)
    in_maps = []
    for c in range(N_CORES):
        m = {"xc": np.ascontiguousarray(x[c * T:(c + 1) * T, :])}
        m.update(weights)
        in_maps.append(m)
    res = run_bass_kernel_spmd(nc, in_maps, core_ids=list(range(N_CORES)))
    out = np.concatenate([r["out"] for r in res.results], axis=0)
    if DBG:
        kernel._dbg = res.results
    return out.reshape(1, S, D)


# version 10
# speedup vs baseline: 1.0191x; 1.0191x over previous
import contextlib
import numpy as np
import ml_dtypes
import concourse.bass as bass
import concourse.mybir as mybir
from concourse.bass_utils import run_bass_kernel_spmd

F32 = mybir.dt.float32
BF16 = mybir.dt.bfloat16
AF = mybir.ActivationFunctionType
ALU = mybir.AluOpType
AX = mybir.AxisListType

N_CORES = 8
T = 1024
D = 4096
HEADS = 32
S = 8192
SCALE = 1.0 / float(np.sqrt(128.0))
DBG = True


def _pairs_mat():
    p = np.zeros((8, 28), np.float32)
    c = 0
    for i in range(8):
        for j in range(i + 1, 8):
            p[i, c] = 1.0
            p[j, c] = 1.0
            c += 1
    return p


class PB:
    ENGINES = ("tensor", "scalar", "vector", "gpsimd", "sync")

    def __init__(self, nc):
        self.nc = nc
        self.ops = {e: [] for e in self.ENGINES}
        self.sem = {}
        self.cnt = {}
        self.waited = {e: {} for e in self.ENGINES}
        self.wait_records = []

    def _sem(self, name):
        if name not in self.sem:
            self.sem[name] = self.nc.alloc_semaphore(name)
            self.cnt[name] = 0
        return self.sem[name]

    def wait(self, eng, sname, val):
        if val is None or val <= 0:
            return
        sem = self._sem(sname)
        w = self.waited[eng]
        if w.get(sname, -1) >= val:
            return
        w[sname] = val
        self.wait_records.append((eng, sname, val))
        self.ops[eng].append(lambda e, sem=sem, val=val: e.wait_ge(sem, val))

    def waitp(self, eng, pair):
        if pair:
            self.wait(eng, pair[0], pair[1])

    def op(self, eng, fn, inc=None, incn=1):
        if inc is None:
            self.ops[eng].append(lambda e, fn=fn: fn(e))
            return None
        sem = self._sem(inc)
        self.cnt[inc] += incn
        self.ops[eng].append(
            lambda e, fn=fn, sem=sem, n=incn: fn(e).then_inc(sem, n))
        return self.cnt[inc]

    def dma(self, eng, out_ap, in_ap, sem=None, **kw):
        sname = sem or ("g" if eng == "gpsimd" else "y")
        n = self.op(eng, lambda e, o=out_ap, i=in_ap, kw=kw: e.dma_start(out=o, in_=i, **kw),
                    inc=sname, incn=16)
        return (sname, n)

    def barrier(self):
        snap = dict(self.cnt)
        for eng in self.ENGINES:
            for sname, val in snap.items():
                self.wait(eng, sname, val)

    def finalize_check(self):
        for eng, sname, val in self.wait_records:
            assert val <= self.cnt.get(sname, 0), (
                f"{eng} waits {sname}>={val} but total incs = {self.cnt.get(sname, 0)}")


def build(nc, DBG=False):
    p = PB(nc)
    for base in ("t", "s", "v", "g", "y", "c1", "c2"):
        p._sem(base)
    xc = nc.dram_tensor("xc", [T, D], F32, kind="ExternalInput").ap()
    qkvw = nc.dram_tensor("qkvw", [D, 3 * D], F32, kind="ExternalInput").ap()
    qkvb = nc.dram_tensor("qkvb", [3 * D], F32, kind="ExternalInput").ap()
    ow = nc.dram_tensor("ow", [D, D], F32, kind="ExternalInput").ap()
    ob = nc.dram_tensor("ob", [D], F32, kind="ExternalInput").ap()
    gate_w = nc.dram_tensor("gate_w", [D, 8], F32, kind="ExternalInput").ap()
    up_w = nc.dram_tensor("up_w", [D, 8192], F32, kind="ExternalInput").ap()
    down_w = nc.dram_tensor("down_w", [8192, D], F32, kind="ExternalInput").ap()
    fc1_w = nc.dram_tensor("fc1_w", [D, 2048], F32, kind="ExternalInput").ap()
    fc2_w = nc.dram_tensor("fc2_w", [2048, D], F32, kind="ExternalInput").ap()
    expert_w = nc.dram_tensor("expert_w", [2048, 2048], F32, kind="ExternalInput").ap()
    ln_g = nc.dram_tensor("ln_g", [D], F32, kind="ExternalInput").ap()
    ln_b = nc.dram_tensor("ln_b", [D], F32, kind="ExternalInput").ap()
    out_d = nc.dram_tensor("out", [T, D], F32, kind="ExternalOutput").ap()

    q_spill = nc.dram_tensor("q_spill", [D, T], BF16, kind="Internal").ap()
    k_in = nc.dram_tensor("k_in", [D, T], BF16, kind="Internal").ap()
    v_in = nc.dram_tensor("v_in", [T, D], BF16, kind="Internal").ap()
    k_g = nc.dram_tensor("k_g", [N_CORES * D, T], BF16, kind="Internal",
                         addr_space="Shared").ap()
    v_g = nc.dram_tensor("v_g", [S, D], BF16, kind="Internal",
                         addr_space="Shared").ap()
    ctx_dram = nc.dram_tensor("ctx_dram", [D, T], BF16, kind="Internal").ap()
    x1_dram = nc.dram_tensor("x1_dram", [T, D], F32, kind="Internal").ap()
    h1_dram = nc.dram_tensor("h1_dram", [T, D], F32, kind="Internal").ap()
    h1fm_dram = nc.dram_tensor("h1fm_dram", [D, T], BF16, kind="Internal").ap()
    pre_dram = nc.dram_tensor("pre_dram", [T, D], F32, kind="Internal").ap()
    knl_d = nc.dram_tensor("knl_d", [1, 32], F32, kind="Internal").ap()
    kng_d = nc.dram_tensor("kng_d", [1, 32], F32, kind="Internal").ap()

    if DBG:
        x1_dbg = nc.dram_tensor("x1_dbg", [T, D], F32, kind="ExternalOutput").ap()
        h1_dbg = nc.dram_tensor("h1_dbg", [T, D], F32, kind="ExternalOutput").ap()
        ctx_dbg = nc.dram_tensor("ctx_dbg", [D, T], BF16, kind="ExternalOutput").ap()
        kin_dbg = nc.dram_tensor("kin_dbg", [D, T], BF16, kind="ExternalOutput").ap()
        vin_dbg = nc.dram_tensor("vin_dbg", [T, D], BF16, kind="ExternalOutput").ap()
        qsp_dbg = nc.dram_tensor("qsp_dbg", [D, T], BF16, kind="ExternalOutput").ap()
        nrm_dbg = nc.dram_tensor("nrm_dbg", [1, 128], F32, kind="ExternalOutput").ap()

    ident_d = nc.inline_tensor(np.eye(128, dtype=ml_dtypes.bfloat16), "ident_const").ap()
    pairs_d = nc.inline_tensor(_pairs_mat(), "pairs_const").ap()

    ones_f = nc.alloc_sbuf_tensor("ones_f", [128, 1], F32)
    ones_bf = nc.alloc_sbuf_tensor("ones_bf", [128, 1], BF16)
    onesr_f = nc.alloc_sbuf_tensor("onesr_f", [1, 128], F32)
    i_bf = nc.alloc_sbuf_tensor("i_bf", [128, 128], BF16)
    p_sb = nc.alloc_sbuf_tensor("p_sb", [8, 28], F32)
    bias_cols = nc.alloc_sbuf_tensor("bias_cols", [128, 96], F32)
    gate_sb = nc.alloc_sbuf_tensor("gate_sb", [128, 32 * 8], BF16)
    gsum_cols = nc.alloc_sbuf_tensor("gsum_cols", [128, 8], F32)
    cols = nc.alloc_sbuf_tensor("cols", [128, 32], F32)
    rowb = nc.alloc_sbuf_tensor("rowb", [1, 2 * 512], F32)
    bias_h = nc.alloc_sbuf_tensor("bias_h", [128, 2], F32)
    bias_const = nc.alloc_sbuf_tensor("bias_const", [128, 1], F32)
    scols = nc.alloc_sbuf_tensor("scols", [128, 8], F32)
    qn2 = nc.alloc_sbuf_tensor("qn2", [1, 64], F32)
    kn2l = nc.alloc_sbuf_tensor("kn2l", [1, 32], F32)
    kn2g = nc.alloc_sbuf_tensor("kn2g", [1, 32], F32)
    vfl = nc.alloc_sbuf_tensor("vfl", [128, 2], F32)
    sfl = nc.alloc_sbuf_tensor("sfl", [128, 2], F32)

    C_SUM, C_SQ, C_MU, C_MSQ, C_MU2, C_VAR, C_SD, C_RSTD = 0, 1, 2, 3, 4, 5, 6, 7
    C_MKA, C_MKB, C_MQ, C_C2, C_BV = 8, 9, 10, 11, 13
    C_SUMB, C_SQB = 14, 15

    def col(j):
        return cols[:, j:j + 1]

    def scol(j):
        return cols[0:1, j:j + 1]

    S_SD, S_SQ, S_C = 0, 1, 2

    def sco(j):
        return scols[:, j:j + 1]

    def vflush():
        return p.op("vector", lambda e: e.tensor_copy(vfl[:, 0:1], vfl[:, 1:2]), inc="v")

    def sflush():
        return p.op("scalar", lambda e: e.copy(sfl[:, 0:1], sfl[:, 1:2]), inc="s")

    def vchain(fn):
        n = p.op("vector", fn, inc="v")
        p.wait("vector", "v", n)
        return n

    def schain(fn):
        n = p.op("scalar", fn, inc="s")
        p.wait("scalar", "s", n)
        return n

    C_NT = 16

    def ln_stats(sumsq_ap):
        vchain(lambda e: e.tensor_scalar_mul(col(C_MU), col(C_SUM), 1.0 / D))
        vchain(lambda e, a=sumsq_ap: e.tensor_scalar_mul(col(C_MSQ), a, 1.0 / D))
        vchain(lambda e: e.tensor_tensor(col(C_MU2), col(C_MU), col(C_MU), ALU.mult))
        vchain(lambda e: e.tensor_tensor(col(C_VAR), col(C_MSQ), col(C_MU2), ALU.subtract))
        vchain(lambda e: e.tensor_scalar_add(col(C_SD), col(C_VAR), 1e-5))
        n = vflush()
        p.wait("scalar", "v", n)
        schain(lambda e: e.sqrt(sco(S_SD), col(C_SD)))
        m = sflush()
        p.wait("vector", "s", m)
        vchain(lambda e: e.reciprocal(col(C_RSTD), sco(S_SD)))
        r = C_RSTD
        for _ in range(2):
            r2 = C_NT if r == C_RSTD else C_RSTD
            vchain(lambda e, r=r: e.tensor_tensor(col(C_NT + 1), col(r), col(r), ALU.mult))
            vchain(lambda e: e.tensor_tensor(col(C_NT + 2), col(C_NT + 1), col(C_SD), ALU.mult))
            vchain(lambda e: e.tensor_scalar(col(C_NT + 1), col(C_NT + 2), -0.5, 1.5,
                                             ALU.mult, ALU.add))
            vchain(lambda e, r=r, r2=r2: e.tensor_tensor(col(r2), col(r), col(C_NT + 1), ALU.mult))
            r = r2
        assert r == C_RSTD
        return p.cnt["v"]

    p.dma("gpsimd", i_bf[:], ident_d, sem="big")
    p.dma("gpsimd", p_sb[:], pairs_d, sem="big")
    gw3 = gate_w.rearrange("(c p) e -> p c e", p=128)
    _, big_init = p.dma("gpsimd", gate_sb[:].rearrange("p (c e) -> p c e", e=8), gw3, sem="big")
    p.op("vector", lambda e: e.memset(ones_f[:], 1.0))
    p.op("vector", lambda e: e.memset(ones_bf[:], 1.0))
    p.op("vector", lambda e: e.memset(bias_const[:], -2.0))
    v_ones = p.op("vector", lambda e: e.memset(onesr_f[:], 1.0), inc="v")

    with nc.psum_tensor("pinit", [128, 512], F32) as pinit:
        s_last = 0
        t_need = 0
        for gidx in range(24):
            slot = gidx % 2
            rb = rowb[0:1, slot * 512:(slot + 1) * 512]
            p.wait("sync", "t", t_need)
            y_row = p.dma("sync", rb, qkvb[gidx * 512:(gidx + 1) * 512].rearrange("(a b) -> a b", a=1),
                          sem=f"rb{slot}")
            p.waitp("tensor", y_row)
            p.wait("tensor", "v", v_ones)
            p.wait("tensor", "s", s_last)
            for j in range(4):
                t_need = p.op("tensor", lambda e, rb=rb, j=j: e.matmul(
                    pinit[:, j:j + 1], rb[0:1, j * 128:(j + 1) * 128], ones_f[0:1, 0:1],
                    start=True, stop=True), inc="t")
            p.wait("scalar", "t", t_need)
            s_last = p.op("scalar", lambda e, gidx=gidx: e.copy(
                bias_cols[:, gidx * 4:(gidx + 1) * 4], pinit[:, 0:4]), inc="s")
        p.wait("tensor", "s", s_last)
        rowb_free_t = t_need
    p.barrier()

    og_order = list(range(32, 64)) + list(range(64, 96)) + list(range(0, 32))

    with (
        nc.sbuf_tensor("x_fm", [128, 32 * 1024], BF16) as x_fm,
        nc.sbuf_tensor("wbuf", [128, 2 * 32 * 512], BF16) as wbuf,
        nc.sbuf_tensor("x_tm", [128, 2 * 4096], BF16) as x_tm,
        nc.sbuf_tensor("stg", [128, 4 * 512], BF16) as stg,
        nc.sbuf_tensor("vstg", [128, 2 * 512], BF16) as vstg,
        nc.sbuf_tensor("nsq", [128, 2 * 512], BF16) as nsq,
        nc.psum_tensor("pnrm", [128, 1024], F32) as pnrm,
        nc.psum_tensor("pqa", [128, 1024], F32) as pqa,
        nc.psum_tensor("pqb", [128, 1024], F32) as pqb,
        nc.psum_tensor("ptr", [128, 2 * 512], F32) as ptr,
    ):
        tt_tdone = [0] * 8
        s_tr = 0
        for tt in range(8):
            slot = tt % 2
            if tt >= 2:
                p.wait("gpsimd", "t", tt_tdone[tt - 2])
            gl = p.dma("gpsimd", x_tm[:, slot * 4096:(slot + 1) * 4096],
                       xc[tt * 128:(tt + 1) * 128, :], sem=f"xl{slot}")
            p.waitp("tensor", gl)
            p.wait("tensor", "big", big_init)
            for fcg in range(8):
                pslot = ptr[:, (fcg % 2) * 512:(fcg % 2 + 1) * 512]
                p.wait("tensor", "s", s_tr)
                for j in range(4):
                    fc = fcg * 4 + j
                    tdone = p.op("tensor", lambda e, slot=slot, fc=fc, j=j, ps=pslot: e.matmul(
                        ps[:, j * 128:(j + 1) * 128],
                        x_tm[:, slot * 4096 + fc * 128: slot * 4096 + fc * 128 + 128],
                        i_bf[:], start=True, stop=True), inc="t")
                p.wait("scalar", "t", tdone)
                dst = x_fm[:].rearrange("q (c t) -> q c t", t=1024)[
                    :, fcg * 4:(fcg + 1) * 4, tt * 128:(tt + 1) * 128]
                src = pslot.rearrange("q (c t) -> q c t", t=128)
                s_tr = p.op("scalar", lambda e, d=dst, sp=src: e.copy(d, sp), inc="s")
            tt_tdone[tt] = tdone
        p.wait("tensor", "s", s_tr)

        nsq_free_t = [0, 0]
        pnrm_free_v = [0, 0]
        wslab_done = [0, 0]
        wslab_load = [None, None]
        slot_free_s = [0, 0, 0, 0]
        stg_free = [None] * 4
        vstg_free = [None, None]
        n_vstores = 0
        ptr_ev = s_tr
        psum_halves = [
            lambda: pqa[:, 0:512], lambda: pqa[:, 512:1024],
            lambda: pqb[:, 0:512], lambda: pqb[:, 512:1024],
        ]
        qkvw3 = qkvw.rearrange("(c p) o -> p c o", p=128)
        si = 0
        sgi = 0
        for grp in range(24):
            ogbase = og_order[grp * 4]
            pbuf = grp % 2
            p.wait("gpsimd", "t", wslab_done[pbuf])
            wdst = wbuf[:, pbuf * 16384:(pbuf + 1) * 16384].rearrange("p (c o) -> p c o", o=512)
            wsrc = qkvw3[:, :, ogbase * 128:ogbase * 128 + 512]
            wslab_load[pbuf] = p.dma("gpsimd", wdst, wsrc, sem=f"w{pbuf}")

            for og_in in range(4):
                og = ogbase + og_in
                for th in range(2):
                    half = psum_halves[si % 4]()
                    p.waitp("tensor", wslab_load[pbuf])
                    p.wait("tensor", "s", slot_free_s[si % 4])
                    for fc in range(32):
                        lt = wbuf[:, pbuf * 16384 + fc * 512 + og_in * 128:
                                  pbuf * 16384 + fc * 512 + og_in * 128 + 128]
                        rh = x_fm[:, fc * 1024 + th * 512: fc * 1024 + (th + 1) * 512]
                        tq = p.op("tensor", lambda e, h=half, lt=lt, rh=rh, f=fc: e.matmul(
                            h, lt, rh, start=(f == 0), stop=(f == 31)), inc="t")
                    wslab_done[pbuf] = tq
                    p.wait("scalar", "t", tq)
                    if og < 64:
                        sem_pfx = "sk" if og >= 32 else "sq"
                        p.waitp("scalar", stg_free[sgi % 4])
                        dstg = stg[:, (sgi % 4) * 512:(sgi % 4 + 1) * 512]
                        sev = p.op("scalar", lambda e, d=dstg, h=half, og=og: e.activation(
                            d, h, AF.Identity, bias=bias_cols[:, og:og + 1], scale=1.0),
                            inc="s")
                        slot_free_s[si % 4] = sev
                        p.wait("sync", "s", sev)
                        if og < 32:
                            tgt = q_spill[og * 128:(og + 1) * 128, th * 512:(th + 1) * 512]
                        else:
                            tgt = k_in[(og - 32) * 128:(og - 32) * 128 + 128, th * 512:(th + 1) * 512]
                        stg_free[sgi % 4] = p.dma("sync", tgt, dstg, sem=f"{sem_pfx}{sgi % 4}")
                        ns = sgi % 2
                        p.wait("scalar", "t", nsq_free_t[ns])
                        snq = p.op("scalar", lambda e, d=dstg, ns=ns: e.activation(
                            nsq[:, ns * 512:(ns + 1) * 512], d, AF.Square), inc="s")
                        p.wait("tensor", "s", snq)
                        p.wait("tensor", "v", pnrm_free_v[ns])
                        tnq = p.op("tensor", lambda e, ns=ns: e.matmul(
                            pnrm[0:1, ns * 512:(ns + 1) * 512], ones_bf[:],
                            nsq[:, ns * 512:(ns + 1) * 512], start=True, stop=True), inc="t")
                        nsq_free_t[ns] = tnq
                        p.wait("vector", "t", tnq)
                        if og < 32:
                            dstn = qn2[0:1, og * 2 + th: og * 2 + th + 1]
                            pnrm_free_v[ns] = p.op("vector", lambda e, d=dstn, ns=ns: e.tensor_reduce(
                                d, pnrm[0:1, ns * 512:(ns + 1) * 512], AX.X, ALU.max), inc="v")
                        else:
                            dstn = cols[0:1, 20 + th: 21 + th]
                            vr_ = p.op("vector", lambda e, d=dstn, ns=ns: e.tensor_reduce(
                                d, pnrm[0:1, ns * 512:(ns + 1) * 512], AX.X, ALU.max), inc="v")
                            pnrm_free_v[ns] = vr_
                            if th == 1:
                                p.wait("vector", "v", vr_)
                                vchain(lambda e, hh=og - 32: e.tensor_tensor(
                                    kn2l[0:1, hh:hh + 1], cols[0:1, 20:21],
                                    cols[0:1, 21:22], ALU.max))
                        sgi += 1
                    else:
                        vslot = n_vstores % 2
                        p.waitp("scalar", vstg_free[vslot])
                        vs1 = vstg[:, vslot * 512:(vslot + 1) * 512]
                        sev = p.op("scalar", lambda e, d=vs1, h=half, og=og: e.activation(
                            d, h, AF.Identity, bias=bias_cols[:, og:og + 1], scale=1.0),
                            inc="s")
                        slot_free_s[si % 4] = sev
                        p.wait("tensor", "s", sev)
                        ptslot = ptr[:, vslot * 512:(vslot + 1) * 512]
                        p.wait("tensor", "s", ptr_ev)
                        for j in range(4):
                            ttr = p.op("tensor", lambda e, ps=ptslot, j=j, vs=vs1: e.matmul(
                                ps[:, j * 128:(j + 1) * 128], vs[:, j * 128:(j + 1) * 128],
                                i_bf[:], start=True, stop=True), inc="t")
                        p.wait("scalar", "t", ttr)
                        ptr_ev = p.op("scalar", lambda e, d=vs1, ps=ptslot: e.copy(d, ps), inc="s")
                        p.wait("sync", "s", ptr_ev)
                        vdst = v_in[th * 512:(th + 1) * 512, (og - 64) * 128:(og - 64) * 128 + 128]
                        vdst3 = vdst.rearrange("(j q) d -> q j d", q=128)
                        vsrc3 = vs1.rearrange("q (j d) -> q j d", d=128)
                        vstg_free[vslot] = p.dma("sync", vdst3, vsrc3, sem=f"sv{vslot}")
                        n_vstores += 1
                    si += 1
            if grp == 15:
                for j in range(2):
                    p.wait("gpsimd", f"sv{j}", p.cnt.get(f"sv{j}", 0))
                p.op("gpsimd", lambda e: e.collective_compute(
                    "AllGather", ALU.bypass, replica_groups=[list(range(N_CORES))],
                    ins=[v_in], outs=[v_g]), inc="c2")
            if grp == 9:
                for j in range(4):
                    p.wait("gpsimd", f"sk{j}", p.cnt.get(f"sk{j}", 0))
                p.op("gpsimd", lambda e: e.collective_compute(
                    "AllGather", ALU.bypass, replica_groups=[list(range(N_CORES))],
                    ins=[k_in], outs=[k_g]), inc="c1")
                vkn = vflush()
                p.wait("sync", "v", vkn)
                ykn = p.dma("sync", knl_d, kn2l[0:1, :], sem="kn")
                p.waitp("gpsimd", ykn)
                p.op("gpsimd", lambda e: e.collective_compute(
                    "AllReduce", ALU.max, replica_groups=[list(range(N_CORES))],
                    ins=[knl_d], outs=[kng_d]), inc="c3")
    p.barrier()

    with (
        nc.sbuf_tensor("kbuf", [128, 2 * 8192], BF16) as kbuf,
        nc.sbuf_tensor("vbuf", [128, 2 * 8192], BF16) as vbuf,
        nc.sbuf_tensor("qb", [128, 2 * 512], BF16) as qb,
        nc.sbuf_tensor("prb", [128, 2 * 1024], BF16) as prb,
        nc.sbuf_tensor("pssum", [128, 512], BF16) as pssum,
        nc.sbuf_tensor("lpp", [128, 4 * 512], F32) as lpp,
        nc.sbuf_tensor("rcp", [1, 512], F32) as rcp,
        nc.sbuf_tensor("rbb", [128, 512], F32) as rbb,
        nc.sbuf_tensor("ctxs", [128, 2 * 512], BF16) as ctxs,
        nc.psum_tensor("pa", [128, 1024], F32) as pa,
        nc.psum_tensor("pb", [128, 1024], F32) as pb,
        nc.psum_tensor("pc", [128, 1024], F32) as pc,
        nc.psum_tensor("pd", [128, 512], F32) as pd,
        nc.psum_tensor("pbias", [128, 512], F32) as pbias,
    ):
        units = [(h, qw) for h in range(HEADS) for qw in range(2)]
        NU = len(units)
        kv_used_t = [0, 0]
        kv_load = [None, None]
        qb_free = [0, 0]
        ctxs_free = [None, None]
        pc_free = [0, 0]
        pd_guard_v = 0
        last_exp = [0, 0]
        last_pv = [0, 0]
        last_ps = [0, 0]
        bias_ready_s = [0] * NU
        yq_u = [None] * NU
        vl_u = [0] * NU
        pv31_u = [0] * NU
        tb_prev_t = 0
        pbias_free_s = [0, 0]
        pend_tail = None
        p.wait("sync", "c3", 1)
        ykng = p.dma("sync", kn2g[0:1, :], kng_d, sem="kn")

        def load_kv(h):
            buf2 = h % 2
            p.wait("sync", "t", kv_used_t[buf2])
            p.wait("sync", "c1", 1)
            p.wait("sync", "c2", 1)
            for r in range(N_CORES):
                ys = p.dma("sync", kbuf[:, buf2 * 8192 + r * 1024: buf2 * 8192 + (r + 1) * 1024],
                           k_g[r * D + h * 128: r * D + h * 128 + 128, :], sem=f"kv{buf2}")
            for r in range(N_CORES):
                src = v_g[r * 1024:(r + 1) * 1024, h * 128:(h + 1) * 128]
                src3 = src.rearrange("(a q) d -> q a d", q=128)
                dst3 = vbuf[:, buf2 * 8192 + r * 1024: buf2 * 8192 + (r + 1) * 1024].rearrange(
                    "q (a d) -> q a d", d=128)
                ys = p.dma("sync", dst3, src3, sem=f"kv{buf2}")
            kv_load[buf2] = ys

        def emit_qload(u):
            h, qw = units[u]
            p.wait("sync", "v", qb_free[qw])
            yq_u[u] = p.dma("sync", qb[:, qw * 512:(qw + 1) * 512],
                            q_spill[h * 128:(h + 1) * 128, qw * 512:(qw + 1) * 512],
                            sem=f"qb{qw}")

        def emit_bound(u):
            nonlocal tb_prev_t
            h, qw = units[u]
            p.waitp("vector", ykng)
            p.wait("vector", "t", tb_prev_t)
            vchain(lambda e, h=h, qw=qw: e.tensor_tensor(
                scol(C_C2), qn2[0:1, h * 2 + qw: h * 2 + qw + 1],
                kn2g[0:1, h:h + 1], ALU.mult))
            vv = vflush()
            p.wait("scalar", "v", vv)
            schain(lambda e: e.sqrt(scols[0:1, S_C:S_C + 1], scol(C_C2)))
            sv = sflush()
            p.wait("vector", "s", sv)
            vchain(lambda e: e.tensor_scalar(
                scol(C_BV), scols[0:1, S_C:S_C + 1], -SCALE, 12.0,
                ALU.mult, ALU.add))
            vv = vflush()
            p.wait("tensor", "v", vv)
            p.wait("tensor", "s", pbias_free_s[u % 2])
            tb = p.op("tensor", lambda e, u=u: e.matmul(
                pbias[:, (u % 2):(u % 2) + 1], onesr_f[0:1, :], scol(C_BV),
                start=True, stop=True), inc="t")
            tb_prev_t = tb
            p.wait("scalar", "t", tb)
            sb_ = p.op("scalar", lambda e, qw=qw, u=u: e.copy(
                bias_h[:, qw:qw + 1], pbias[:, (u % 2):(u % 2) + 1]), inc="s")
            bias_ready_s[u] = sb_
            pbias_free_s[u % 2] = sb_

        def emit_tail(u):
            nonlocal pd_guard_v
            h, qw = units[u]
            base = (u % 2) * 2
            pcs_t = pc[:, (u % 2) * 512:(u % 2 + 1) * 512]
            p.wait("tensor", "v", vl_u[u])
            p.wait("tensor", "v", pd_guard_v)
            tl = p.op("tensor", lambda e, base=base: e.matmul(
                pd[0:1, 0:512], ones_f[:], lpp[:, base * 512:(base + 1) * 512],
                start=True, stop=True), inc="t")
            p.wait("vector", "t", tl)
            vr = p.op("vector", lambda e: e.reciprocal(rcp[:], pd[0:1, 0:512]), inc="v")
            p.wait("tensor", "v", vr)
            tbc = p.op("tensor", lambda e: e.matmul(
                pd[:], onesr_f[0:1, :], rcp[0:1, :], start=True, stop=True), inc="t")
            p.wait("vector", "t", tbc)
            p.op("vector", lambda e: e.tensor_copy(rbb[:], pd[:]))
            p.wait("vector", "t", pv31_u[u])
            p.waitp("vector", ctxs_free[qw])
            vctx = p.op("vector", lambda e, qw=qw, pcs_t=pcs_t: e.tensor_tensor(
                ctxs[:, qw * 512:(qw + 1) * 512], pcs_t, rbb[:], ALU.mult), inc="v")
            pc_free[u % 2] = vctx
            pd_guard_v = vctx
            qb_free[qw] = vctx
            p.wait("sync", "v", vctx)
            ctxs_free[qw] = p.dma(
                "sync", ctx_dram[h * 128:(h + 1) * 128, qw * 512:(qw + 1) * 512],
                ctxs[:, qw * 512:(qw + 1) * 512], sem=f"cx{qw}")

        emit_qload(0)
        emit_bound(0)
        for u, (h, qw) in enumerate(units):
            buf = h % 2
            if qw == 0:
                if h == 0:
                    load_kv(0)
                    load_kv(1)
                elif h < HEADS - 1:
                    load_kv(h + 1)
                p.waitp("tensor", kv_load[buf])
            p.waitp("tensor", yq_u[u])
            base = (u % 2) * 2
            vl = p.op("vector", lambda e, base=base: e.memset(
                lpp[:, base * 512:(base + 1) * 512], 0.0), inc="v")
            pcs = pc[:, (u % 2) * 512:(u % 2 + 1) * 512]

            def emit_pv(pq, first_wait):
                parq = pq % 2
                p.wait("tensor", "s", last_exp[parq])
                if first_wait:
                    p.wait("tensor", "v", pc_free[u % 2])
                for jj in range(2):
                    kt = 2 * pq + jj
                    tpv = p.op("tensor", lambda e, jj=jj, kt=kt, buf=buf, parq=parq, pq=pq, pcs=pcs: e.matmul(
                        pcs, vbuf[:, buf * 8192 + kt * 128: buf * 8192 + kt * 128 + 128],
                        prb[:, parq * 1024 + jj * 512: parq * 1024 + (jj + 1) * 512],
                        start=(pq == 0 and jj == 0), stop=(pq == 31 and jj == 1),
                        skip_group_check=True), inc="t")
                last_pv[parq] = tpv

            p.wait("scalar", "s", bias_ready_s[u])
            for pp in range(32):
                par = pp % 2
                pair = pa if par == 0 else pb
                p.wait("tensor", "s", last_exp[par])
                for jj in range(2):
                    kt = 2 * pp + jj
                    tsc = p.op("tensor", lambda e, pr=pair, jj=jj, kt=kt, buf=buf, qw=qw: e.matmul(
                        pr[:, jj * 512:(jj + 1) * 512],
                        kbuf[:, buf * 8192 + kt * 128: buf * 8192 + kt * 128 + 128],
                        qb[:, qw * 512:(qw + 1) * 512],
                        start=True, stop=True), inc="t")
                p.wait("scalar", "t", tsc)
                p.wait("scalar", "t", last_pv[par])
                p.wait("scalar", "v", last_ps[par])
                se = p.op("scalar", lambda e, pr=pair, par=par, qw=qw: e.activation(
                    prb[:, par * 1024:(par + 1) * 1024], pr[:],
                    AF.Exp, bias=bias_h[:, qw:qw + 1], scale=SCALE), inc="s")
                last_exp[par] = se
                if pp >= 1:
                    emit_pv(pp - 1, pp == 1)
                p.wait("vector", "s", se)
                p.op("vector", lambda e, par=par: e.tensor_tensor(
                    pssum[:], prb[:, par * 1024:par * 1024 + 512],
                    prb[:, par * 1024 + 512:(par + 1) * 1024], ALU.add))
                vl = p.op("vector", lambda e, pp=pp, base=base: e.tensor_tensor(
                    lpp[:, (base + (pp + 1) % 2) * 512:(base + (pp + 1) % 2 + 1) * 512],
                    lpp[:, (base + pp % 2) * 512:(base + pp % 2) * 512 + 512],
                    pssum[:], ALU.add), inc="v")
                last_ps[par] = vl
                if pp == 2 and u + 1 < NU:
                    emit_bound(u + 1)
                if pp == 4 and pend_tail is not None:
                    emit_tail(pend_tail)
                    pend_tail = None
                if pp == 6 and u + 1 < NU:
                    emit_qload(u + 1)
            emit_pv(31, False)
            pv31_u[u] = last_pv[1]
            vl_u[u] = vl
            if qw == 1:
                kv_used_t[buf] = last_pv[1]
            pend_tail = u
        emit_tail(NU - 1)
    p.barrier()

    with (
        nc.sbuf_tensor("ctx_fm", [128, 32 * 1024], BF16) as ctx_fm,
        nc.sbuf_tensor("owbuf", [128, 2 * 32 * 512], BF16) as owbuf,
        nc.sbuf_tensor("ob_bc", [128, 4096], F32) as ob_bc,
        nc.sbuf_tensor("xres", [128, 2 * 512], F32) as xres,
        nc.sbuf_tensor("stga", [128, 2 * 512], F32) as stga,
        nc.sbuf_tensor("stgb", [128, 2 * 512], F32) as stgb,
        nc.psum_tensor("po", [128, 1024], F32) as po,
        nc.psum_tensor("pi4", [128, 512], F32) as pi4,
    ):
        t_ob = rowb_free_t
        s_ob = 0
        for gidx in range(8):
            slotr = gidx % 2
            rb = rowb[0:1, slotr * 512:(slotr + 1) * 512]
            p.wait("sync", "t", t_ob)
            yr = p.dma("sync", rb, ob[gidx * 512:(gidx + 1) * 512].rearrange("(a b) -> a b", a=1),
                       sem=f"rb{slotr}")
            p.waitp("tensor", yr)
            p.wait("tensor", "s", s_ob)
            t_ob = p.op("tensor", lambda e, rb=rb: e.matmul(
                pi4[:], onesr_f[0:1, :], rb, start=True, stop=True), inc="t")
            p.wait("scalar", "t", t_ob)
            s_ob = p.op("scalar", lambda e, gidx=gidx: e.copy(
                ob_bc[:, gidx * 512:(gidx + 1) * 512], pi4[:]), inc="s")
        rowb_free_t = t_ob
        p.wait("sync", "cx0", p.cnt.get("cx0", 0))
        p.wait("sync", "cx1", p.cnt.get("cx1", 0))
        ctx3 = ctx_dram.rearrange("(c p) t -> p c t", p=128)
        yctx = p.dma("sync", ctx_fm[:].rearrange("p (c t) -> p c t", t=1024), ctx3, sem="big")
        ow3 = ow.rearrange("(c p) o -> p c o", p=128)
        ow_done_t = [0, 0]
        ow_load = [None, None]
        po_free = [0, 0]
        xres_y = [None, None]
        xres_free = [0, 0]
        x1st_free = [None, None]
        for ofg in range(8):
            pbuf = ofg % 2
            p.wait("gpsimd", "t", ow_done_t[pbuf])
            wdst = owbuf[:, pbuf * 16384:(pbuf + 1) * 16384].rearrange("p (c o) -> p c o", o=512)
            ow_load[pbuf] = p.dma("gpsimd", wdst, ow3[:, :, ofg * 512:(ofg + 1) * 512],
                                  sem=f"w{pbuf}")
            for tt in range(8):
                pslot = po[:, (tt % 2) * 512:(tt % 2 + 1) * 512]
                p.waitp("tensor", ow_load[pbuf])
                p.waitp("tensor", yctx)
                p.wait("tensor", "v", po_free[tt % 2])
                p.wait("tensor", "s", s_ob)
                for cf in range(32):
                    tof = p.op("tensor", lambda e, ps=pslot, cf=cf, tt=tt, pbuf=pbuf: e.matmul(
                        ps, ctx_fm[:, cf * 1024 + tt * 128: cf * 1024 + tt * 128 + 128],
                        owbuf[:, pbuf * 16384 + cf * 512:pbuf * 16384 + (cf + 1) * 512],
                        start=(cf == 0), stop=(cf == 31)), inc="t")
                ow_done_t[pbuf] = tof
                xslot = tt % 2
                p.wait("sync", "v", xres_free[xslot])
                xres_y[xslot] = p.dma("sync", xres[:, xslot * 512:(xslot + 1) * 512],
                                      xc[tt * 128:(tt + 1) * 128, ofg * 512:(ofg + 1) * 512],
                                      sem=f"xr{xslot}")
                p.wait("vector", "t", tof)
                p.waitp("vector", xres_y[xslot])
                p.waitp("vector", x1st_free[tt % 2])
                p.op("vector", lambda e, ps=pslot, tt=tt, ofg=ofg: e.tensor_tensor(
                    stga[:, (tt % 2) * 512:(tt % 2 + 1) * 512], ps,
                    ob_bc[:, ofg * 512:(ofg + 1) * 512], ALU.add))
                vev = p.op("vector", lambda e, tt=tt, xslot=xslot: e.tensor_tensor(
                    stgb[:, (tt % 2) * 512:(tt % 2 + 1) * 512],
                    stga[:, (tt % 2) * 512:(tt % 2 + 1) * 512],
                    xres[:, xslot * 512:(xslot + 1) * 512], ALU.add), inc="v")
                po_free[tt % 2] = vev
                xres_free[xslot] = vev
                p.wait("sync", "v", vev)
                x1st_free[tt % 2] = p.dma(
                    "sync", x1_dram[tt * 128:(tt + 1) * 128, ofg * 512:(ofg + 1) * 512],
                    stgb[:, (tt % 2) * 512:(tt % 2 + 1) * 512], sem=f"xs{tt % 2}")
    p.barrier()

    with (
        nc.sbuf_tensor("h1_fm", [128, 32 * 1024], BF16) as h1_fm,
        nc.sbuf_tensor("x1t", [128, 4096], F32) as x1t,
        nc.sbuf_tensor("h1t", [128, 4096], F32) as h1t,
        nc.sbuf_tensor("h1bf", [128, 4096], BF16) as h1bf,
        nc.sbuf_tensor("sq4", [128, 4096], F32) as sq4,
        nc.psum_tensor("pt4", [128, 2 * 512], F32) as pt4,
    ):
        x1_cons_s = 0
        x1_cons_v = 0
        sq4_cons_v = 0
        h1bf_t = 0
        last_tr_s = 0
        h1t_st = None
        for tt in range(8):
            p.wait("sync", "s", x1_cons_s)
            p.wait("sync", "v", x1_cons_v)
            yx1 = p.dma("sync", x1t[:], x1_dram[tt * 128:(tt + 1) * 128, :], sem="xl0")
            p.waitp("scalar", yx1)
            p.wait("scalar", "v", sq4_cons_v)
            schain(lambda e: e.activation(sq4[:], x1t[:], AF.Square,
                                          accum_out=sco(S_SQ)))
            ssq = sflush()
            x1_cons_s = ssq
            p.waitp("vector", yx1)
            vchain(lambda e: e.tensor_reduce(col(C_SUM), x1t[:], AX.X, ALU.add))
            p.wait("vector", "s", ssq)
            sq4_cons_v = ln_stats(sco(S_SQ))
            p.waitp("vector", h1t_st)
            vap = p.op("vector", lambda e: e.tensor_scalar(
                h1t[:], x1t[:], col(C_MU), col(C_RSTD), ALU.subtract, ALU.mult), inc="v")
            p.wait("vector", "t", h1bf_t)
            v_apply = p.op("vector", lambda e: e.tensor_copy(h1bf[:], h1t[:]), inc="v")
            x1_cons_v = v_apply
            p.wait("sync", "v", v_apply)
            h1t_st = p.dma("sync", h1_dram[tt * 128:(tt + 1) * 128, :], h1t[:], sem="hs")
            p.wait("tensor", "v", v_apply)
            for fcg in range(8):
                ptsl = pt4[:, (fcg % 2) * 512:(fcg % 2 + 1) * 512]
                p.wait("tensor", "s", last_tr_s)
                for j in range(4):
                    fc = fcg * 4 + j
                    h1bf_t = p.op("tensor", lambda e, ps=ptsl, fc=fc, j=j: e.matmul(
                        ps[:, j * 128:(j + 1) * 128], h1bf[:, fc * 128:(fc + 1) * 128],
                        i_bf[:], start=True, stop=True), inc="t")
                p.wait("scalar", "t", h1bf_t)
                dst = h1_fm[:].rearrange("q (c t) -> q c t", t=1024)[
                    :, fcg * 4:(fcg + 1) * 4, tt * 128:(tt + 1) * 128]
                src = ptsl.rearrange("q (c t) -> q c t", t=128)
                last_tr_s = p.op("scalar", lambda e, d=dst, sp=src: e.copy(d, sp), inc="s")
        p.wait("sync", "s", last_tr_s)
        for fc in range(32):
            p.dma("sync", h1fm_dram[fc * 128:(fc + 1) * 128, :], h1_fm[:, fc * 1024:(fc + 1) * 1024])
        h1fm_y = p.cnt["y"]
    p.barrier()

    up3 = up_w.rearrange("(c p) o -> p c o", p=128)
    down3 = down_w.rearrange("(c p) o -> p c o", p=128)
    fc13 = fc1_w.rearrange("(c p) o -> p c o", p=128)
    fc23 = fc2_w.rearrange("(c p) o -> p c o", p=128)
    ex3 = expert_w.rearrange("(c p) o -> p c o", p=128)
    h1fm3 = h1fm_dram.rearrange("(c p) t -> p c t", p=128)

    with (
        nc.sbuf_tensor("hfm", [128, 64 * 512], BF16) as hfm,
        nc.sbuf_tensor("wb5", [128, 2 * 8192], BF16) as wb5,
        nc.sbuf_tensor("efm", [128, 16 * 512], BF16) as efm,
        nc.sbuf_tensor("g5", [8, 512], F32) as g5,
        nc.sbuf_tensor("rl5", [128, 512], F32) as rl5,
        nc.sbuf_tensor("ssc", [128, 512], F32) as ssc,
        nc.sbuf_tensor("ost", [128, 2 * 512], F32) as ost,
        nc.psum_tensor("pu", [128, 1024], F32) as pu,
        nc.psum_tensor("pu2", [128, 1024], F32) as pu2,
        nc.psum_tensor("pf2", [128, 512], F32) as pf2,
        nc.psum_tensor("pg5", [128, 512], F32) as pg5,
    ):
        p_ttl = [lambda: pu[:, 0:512], lambda: pu[:, 512:1024],
                 lambda: pu2[:, 0:512], lambda: pu2[:, 512:1024]]
        wb5_free_t = [0, 0]
        fc2b_free_t = [0, 0]
        dnx_free_t = [0, 0]
        pslot_free_s = [0, 0, 0, 0]
        pf2_free_v = 0
        ssc_free_v = 0
        ost_free = [None, None]
        g5_cons_t = 0
        scols_cons_carry = 0
        epi_done_v = 0
        hfm_cons_t = 0
        efm_cons_t = 0
        h1w_cons_t = 0
        for w in range(2):
            with nc.sbuf_tensor(f"h1w{w}", [128, 32 * 512], BF16) as h1w:
                p.wait("sync", "y", h1fm_y)
                p.wait("sync", "t", h1w_cons_t)
                p.wait("sync", "v", epi_done_v)
                yh1w = p.dma("sync", h1w[:].rearrange("p (c t) -> p c t", t=512),
                             h1fm3[:, :, w * 512:(w + 1) * 512], sem="big")
                slotpu = 0
                wl = None
                pb5 = 0
                for lc in range(16):
                    if lc % 2 == 0:
                        pb5 = (lc // 2) % 2
                        p.wait("gpsimd", "t", wb5_free_t[pb5])
                        wdst = wb5[:, pb5 * 8192:(pb5 + 1) * 8192].rearrange("p (c o) -> p c o", o=256)
                        wl = p.dma("gpsimd", wdst, fc13[:, :, (lc // 2) * 256:(lc // 2) * 256 + 256],
                                   sem=f"w{pb5}")
                    sl = slotpu % 4
                    ps = p_ttl[sl]()
                    p.waitp("tensor", wl)
                    p.waitp("tensor", yh1w)
                    p.wait("tensor", "s", pslot_free_s[sl])
                    for fc in range(32):
                        lt = wb5[:, pb5 * 8192 + fc * 256 + (lc % 2) * 128:
                                 pb5 * 8192 + fc * 256 + (lc % 2) * 128 + 128]
                        tfc1 = p.op("tensor", lambda e, ps=ps, lt=lt, fc=fc: e.matmul(
                            ps, lt, h1w[:, fc * 512:(fc + 1) * 512],
                            start=(fc == 0), stop=(fc == 31)), inc="t")
                    wb5_free_t[pb5] = tfc1
                    p.wait("scalar", "t", tfc1)
                    p.wait("scalar", "t", hfm_cons_t)
                    pslot_free_s[sl] = p.op("scalar", lambda e, lc=lc, ps=ps: e.copy(
                        hfm[:, lc * 512:(lc + 1) * 512], ps), inc="s")
                    slotpu += 1
                lat_ready_s = max(pslot_free_s)
                for mc in range(16):
                    if mc % 2 == 0:
                        pb5 = (mc // 2) % 2
                        p.wait("gpsimd", "t", wb5_free_t[pb5])
                        wdst = wb5[:, pb5 * 8192:pb5 * 8192 + 4096].rearrange("p (c o) -> p c o", o=256)
                        wl = p.dma("gpsimd", wdst, ex3[:, :, (mc // 2) * 256:(mc // 2) * 256 + 256],
                                   sem=f"w{pb5}")
                    sl = slotpu % 4
                    ps = p_ttl[sl]()
                    p.waitp("tensor", wl)
                    p.wait("tensor", "s", lat_ready_s)
                    p.wait("tensor", "s", pslot_free_s[sl])
                    for lc in range(16):
                        lt = wb5[:, pb5 * 8192 + lc * 256 + (mc % 2) * 128:
                                 pb5 * 8192 + lc * 256 + (mc % 2) * 128 + 128]
                        tex = p.op("tensor", lambda e, ps=ps, lt=lt, lc=lc: e.matmul(
                            ps, lt, hfm[:, lc * 512:(lc + 1) * 512],
                            start=(lc == 0), stop=(lc == 15)), inc="t")
                    wb5_free_t[pb5] = tex
                    p.wait("scalar", "t", tex)
                    p.wait("scalar", "t", efm_cons_t)
                    pslot_free_s[sl] = p.op("scalar", lambda e, mc=mc, ps=ps: e.copy(
                        efm[:, mc * 512:(mc + 1) * 512], ps), inc="s")
                    slotpu += 1
                efm_ready_s = max(pslot_free_s)
                lat_last_read_t = wb5_free_t[pb5]
                p.waitp("tensor", yh1w)
                p.wait("tensor", "big", big_init)
                p.wait("tensor", "s", g5_cons_t * 0)
                for fc in range(32):
                    tg = p.op("tensor", lambda e, fc=fc: e.matmul(
                        pg5[0:8, :], gate_sb[:, fc * 8:(fc + 1) * 8], h1w[:, fc * 512:(fc + 1) * 512],
                        start=(fc == 0), stop=(fc == 31)), inc="t")
                p.wait("scalar", "t", tg)
                p.op("scalar", lambda e: e.copy(g5[:], pg5[0:8, :]))
                sg5 = sflush()
                vgs = 0
                for ttl in range(4):
                    p.wait("tensor", "s", sg5)
                    p.wait("tensor", "v", vgs)
                    p.wait("tensor", "v", pf2_free_v)
                    tpp = p.op("tensor", lambda e, ttl=ttl: e.matmul(
                        pf2[:, 0:28], g5[0:8, ttl * 128:(ttl + 1) * 128], p_sb[:],
                        start=True, stop=True), inc="t")
                    p.wait("vector", "t", tpp)
                    vgs = p.op("vector", lambda e, w=w, ttl=ttl: e.tensor_reduce(
                        gsum_cols[:, w * 4 + ttl: w * 4 + ttl + 1], pf2[:, 0:28],
                        AX.X, ALU.max), inc="v")
                pf2_free_v = vgs
                gs_ready_v = vgs
                g5_cons_t = tpp
                rl_free_s = 0
                for ic in range(64):
                    if ic % 2 == 0:
                        pb5 = (ic // 2) % 2
                        p.wait("gpsimd", "t", wb5_free_t[pb5])
                        wdst = wb5[:, pb5 * 8192:(pb5 + 1) * 8192].rearrange("p (c o) -> p c o", o=256)
                        wl = p.dma("gpsimd", wdst, up3[:, :, (ic // 2) * 256:(ic // 2) * 256 + 256],
                                   sem=f"w{pb5}")
                    sl = slotpu % 4
                    ps = p_ttl[sl]()
                    p.waitp("tensor", wl)
                    p.wait("tensor", "s", pslot_free_s[sl])
                    for fc in range(32):
                        lt = wb5[:, pb5 * 8192 + fc * 256 + (ic % 2) * 128:
                                 pb5 * 8192 + fc * 256 + (ic % 2) * 128 + 128]
                        tup = p.op("tensor", lambda e, ps=ps, lt=lt, fc=fc: e.matmul(
                            ps, lt, h1w[:, fc * 512:(fc + 1) * 512],
                            start=(fc == 0), stop=(fc == 31)), inc="t")
                    wb5_free_t[pb5] = tup
                    p.wait("scalar", "t", tup)
                    p.wait("scalar", "s", rl_free_s)
                    if ic < 16:
                        p.wait("scalar", "t", lat_last_read_t)
                    p.op("scalar", lambda e, ps=ps: e.activation(rl5[:], ps, AF.Relu))
                    pslot_free_s[sl] = p.op("scalar", lambda e, ic=ic: e.activation(
                        hfm[:, ic * 512:(ic + 1) * 512], rl5[:], AF.Square), inc="s")
                    rl_free_s = pslot_free_s[sl]
                    slotpu += 1
                hfm_ready_s = max(pslot_free_s)
                h1w_cons_t = max(wb5_free_t + [tg])
            with (
                nc.sbuf_tensor(f"downx{w}", [128, 2 * 8192], BF16) as dnx,
                nc.sbuf_tensor(f"fc2b{w}", [128, 2 * 8192], BF16) as fc2b,
            ):
                pttl_stop_t = [0, 0, 0, 0]
                for hfg in range(8):
                    fcb = hfg % 2
                    p.wait("gpsimd", "t", fc2b_free_t[fcb])
                    wdst = fc2b[:, fcb * 8192:(fcb + 1) * 8192].rearrange("p (c o) -> p c o", o=512)
                    yfc2 = p.dma("gpsimd", wdst, fc23[:, :, hfg * 512:(hfg + 1) * 512],
                                 sem=f"f2{fcb}")
                    qslots = [wb5[:, 0:8192], wb5[:, 8192:16384], dnx[:, 0:8192], dnx[:, 8192:16384]]
                    yq4 = [None] * 4
                    for iq in range(4):
                        if iq < 2:
                            p.wait("gpsimd", "t", wb5_free_t[iq])
                        else:
                            p.wait("gpsimd", "t", dnx_free_t[iq - 2])
                            p.wait("gpsimd", "t", h1w_cons_t)
                        wdst2 = qslots[iq].rearrange("p (c o) -> p c o", o=512)
                        semn = f"w{iq}" if iq < 2 else f"dn{iq - 2}"
                        yq4[iq] = p.dma("gpsimd", wdst2,
                                        down3[:, iq * 16:(iq + 1) * 16, hfg * 512:(hfg + 1) * 512],
                                        sem=semn)
                    tdn = 0
                    for iq in range(4):
                        p.waitp("tensor", yq4[iq])
                        if iq == 0:
                            p.wait("tensor", "s", hfm_ready_s)
                            for ttl in range(4):
                                p.wait("tensor", "s", pslot_free_s[ttl])
                        for ic in range(16):
                            i_ch = iq * 16 + ic
                            for ttl in range(4):
                                tdn = p.op("tensor", lambda e, iq=iq, ic=ic, ttl=ttl, i_ch=i_ch: e.matmul(
                                    p_ttl[ttl](),
                                    hfm[:, i_ch * 512 + ttl * 128: i_ch * 512 + ttl * 128 + 128],
                                    qslots[iq][:, ic * 512:(ic + 1) * 512],
                                    start=(i_ch == 0), stop=(i_ch == 63),
                                    skip_group_check=True), inc="t")
                                if i_ch == 63:
                                    pttl_stop_t[ttl] = tdn
                        if iq < 2:
                            wb5_free_t[iq] = tdn
                        else:
                            dnx_free_t[iq - 2] = tdn
                    hfm_cons_t = tdn
                    for ttl in range(4):
                        p.waitp("tensor", yfc2)
                        p.wait("tensor", "s", efm_ready_s)
                        p.wait("tensor", "v", pf2_free_v)
                        for mc in range(16):
                            tf2 = p.op("tensor", lambda e, mc=mc, ttl=ttl, fcb=fcb: e.matmul(
                                pf2[:], efm[:, mc * 512 + ttl * 128: mc * 512 + ttl * 128 + 128],
                                fc2b[:, fcb * 8192 + mc * 512: fcb * 8192 + (mc + 1) * 512],
                                start=(mc == 0), stop=(mc == 15)), inc="t")
                        fc2b_free_t[fcb] = tf2
                        efm_cons_t = tf2
                        p.wait("scalar", "t", pttl_stop_t[ttl])
                        p.wait("scalar", "v", ssc_free_v)
                        sdc = p.op("scalar", lambda e, ttl=ttl: e.copy(ssc[:], p_ttl[ttl]()), inc="s")
                        pslot_free_s[ttl] = sdc
                        oslot = (hfg * 4 + ttl) % 2
                        p.wait("vector", "t", tf2)
                        p.wait("vector", "s", sdc)
                        p.wait("vector", "v", gs_ready_v)
                        p.waitp("vector", ost_free[oslot])
                        vpre = p.op("vector", lambda e, w=w, ttl=ttl, oslot=oslot: e.scalar_tensor_tensor(
                            ost[:, oslot * 512:(oslot + 1) * 512], pf2[:],
                            gsum_cols[:, w * 4 + ttl: w * 4 + ttl + 1], ssc[:],
                            ALU.mult, ALU.add), inc="v")
                        pf2_free_v = vpre
                        ssc_free_v = vpre
                        p.wait("sync", "v", vpre)
                        ost_free[oslot] = p.dma(
                            "sync",
                            pre_dram[w * 512 + ttl * 128: w * 512 + (ttl + 1) * 128,
                                     hfg * 512:(hfg + 1) * 512],
                            ost[:, oslot * 512:(oslot + 1) * 512], sem=f"os{oslot}")
            with contextlib.ExitStack() as _epi:
                gbc = _epi.enter_context(nc.sbuf_tensor(f"gbc{w}", [128, 4096], F32))
                bbc = _epi.enter_context(nc.sbuf_tensor(f"bbc{w}", [128, 4096], F32))
                rch = _epi.enter_context(nc.sbuf_tensor(f"rch{w}", [128, 2 * 512], F32))
                sch = _epi.enter_context(nc.sbuf_tensor(f"sch{w}", [128, 2 * 512], F32))
                h1r = _epi.enter_context(nc.sbuf_tensor(f"h1r{w}", [128, 2 * 512], F32))
                z_sb = _epi.enter_context(nc.sbuf_tensor(f"zsb{w}", [128, 4096], F32))
                pe5 = _epi.enter_context(nc.psum_tensor(f"pe5{w}", [128, 512], F32))
                s_bc = 0
                t_bc = rowb_free_t
                for which, (src_ap, dstt) in enumerate(((ln_g, gbc), (ln_b, bbc))):
                    for gidx in range(8):
                        slotr = gidx % 2
                        rb = rowb[0:1, slotr * 512:(slotr + 1) * 512]
                        p.wait("sync", "t", t_bc)
                        yr = p.dma("sync", rb,
                                   src_ap[gidx * 512:(gidx + 1) * 512].rearrange("(a b) -> a b", a=1),
                                   sem=f"rb{slotr}")
                        p.waitp("tensor", yr)
                        p.wait("tensor", "s", s_bc)
                        p.wait("tensor", "t", hfm_cons_t)
                        t_bc = p.op("tensor", lambda e, rb=rb: e.matmul(
                            pe5[:], onesr_f[0:1, :], rb, start=True, stop=True), inc="t")
                        p.wait("scalar", "t", t_bc)
                        s_bc = p.op("scalar", lambda e, d=dstt, gidx=gidx: e.copy(
                            d[:, gidx * 512:(gidx + 1) * 512], pe5[:]), inc="s")
                rowb_free_t = t_bc
                bc_ready = s_bc
                last_v = 0
                scols_cons = scols_cons_carry
                for ttl in range(4):
                    trow = w * 512 + ttl * 128
                    p.wait("sync", "os0", p.cnt.get("os0", 0))
                    p.wait("sync", "os1", p.cnt.get("os1", 0))
                    rload = [None, None]
                    rfree = [0, 0]
                    ssq_all = 0
                    for c in range(8):
                        sl = c % 2
                        p.wait("sync", "v", rfree[sl])
                        rload[sl] = p.dma("sync", rch[:, sl * 512:(sl + 1) * 512],
                                          pre_dram[trow:trow + 128, c * 512:(c + 1) * 512],
                                          sem=f"rc{sl}")
                        p.waitp("scalar", rload[sl])
                        p.wait("scalar", "v", scols_cons)
                        fn = lambda e, sl=sl, c=c: e.activation(
                            rl5[:], rch[:, sl * 512:(sl + 1) * 512], AF.Square,
                            accum_out=scols[:, c:c + 1])
                        if c == 7:
                            schain(fn)
                            ssq_all = sflush()
                        else:
                            p.op("scalar", fn)
                        p.waitp("vector", rload[sl])
                        rfree[sl] = p.op("vector", lambda e, sl=sl, c=c: e.tensor_reduce(
                            cols[:, 16 + c:17 + c], rch[:, sl * 512:(sl + 1) * 512],
                            AX.X, ALU.add), inc="v")
                    p.wait("vector", "v", rfree[1])
                    vchain(lambda e: e.tensor_reduce(col(C_SUM), cols[:, 16:24], AX.X, ALU.add))
                    p.wait("vector", "s", ssq_all)
                    vchain(lambda e: e.tensor_reduce(col(C_SQ), scols[:, 0:8], AX.X, ALU.add))
                    scols_cons = ln_stats(col(C_SQ))
                    hfree = [0, 0]
                    ssq_all = 0
                    for c in range(8):
                        sl = c % 2
                        p.wait("sync", "v", rfree[sl])
                        rload[sl] = p.dma("sync", rch[:, sl * 512:(sl + 1) * 512],
                                          pre_dram[trow:trow + 128, c * 512:(c + 1) * 512],
                                          sem=f"rc{sl}")
                        p.wait("sync", "v", hfree[sl])
                        yh = p.dma("sync", h1r[:, sl * 512:(sl + 1) * 512],
                                   h1_dram[trow:trow + 128, c * 512:(c + 1) * 512],
                                   sem=f"hr{sl}")
                        p.waitp("vector", yh)
                        p.waitp("vector", rload[sl])
                        p.wait("vector", "s", bc_ready)
                        p.op("vector", lambda e, sl=sl: e.tensor_scalar(
                            sch[:, 0:512], rch[:, sl * 512:(sl + 1) * 512],
                            col(C_MU), col(C_RSTD), ALU.subtract, ALU.mult))
                        p.op("vector", lambda e, c=c: e.tensor_tensor(
                            sch[:, 512:1024], sch[:, 0:512], gbc[:, c * 512:(c + 1) * 512], ALU.mult))
                        p.op("vector", lambda e, c=c: e.tensor_tensor(
                            sch[:, 0:512], sch[:, 512:1024], bbc[:, c * 512:(c + 1) * 512], ALU.add))
                        vz = vchain(lambda e, sl=sl, c=c: e.tensor_tensor(
                            z_sb[:, c * 512:(c + 1) * 512], sch[:, 0:512],
                            h1r[:, sl * 512:(sl + 1) * 512], ALU.add))
                        hfree[sl] = vz
                        vzr = p.op("vector", lambda e, c=c: e.tensor_reduce(
                            cols[:, 16 + c:17 + c], z_sb[:, c * 512:(c + 1) * 512],
                            AX.X, ALU.add), inc="v")
                        rfree[sl] = vzr
                        p.wait("scalar", "v", vz)
                        fn = lambda e, c=c: e.activation(
                            rl5[:], z_sb[:, c * 512:(c + 1) * 512], AF.Square,
                            accum_out=scols[:, c:c + 1])
                        if c == 7:
                            schain(fn)
                            ssq_all = sflush()
                        else:
                            p.op("scalar", fn)
                    p.wait("vector", "v", rfree[1])
                    vchain(lambda e: e.tensor_reduce(col(C_SUM), cols[:, 16:24], AX.X, ALU.add))
                    p.wait("vector", "s", ssq_all)
                    vchain(lambda e: e.tensor_reduce(col(C_SQ), scols[:, 0:8], AX.X, ALU.add))
                    scols_cons = ln_stats(col(C_SQ))
                    ofree = [None, None]
                    for c in range(8):
                        sl = c % 2
                        p.waitp("vector", ofree[sl])
                        vo = p.op("vector", lambda e, sl=sl, c=c: e.tensor_scalar(
                            ost[:, sl * 512:(sl + 1) * 512], z_sb[:, c * 512:(c + 1) * 512],
                            col(C_MU), col(C_RSTD), ALU.subtract, ALU.mult), inc="v")
                        last_v = vo
                        p.wait("sync", "v", vo)
                        ofree[sl] = p.dma("sync", out_d[trow:trow + 128, c * 512:(c + 1) * 512],
                                          ost[:, sl * 512:(sl + 1) * 512], sem=f"ot{sl}")
                epi_done_v = last_v
                scols_cons_carry = scols_cons
            p.barrier()

    if DBG:
        for sname in list(p.cnt.keys()):
            p.wait("sync", sname, p.cnt[sname])
        p.dma("sync", x1_dbg, x1_dram)
        p.dma("sync", h1_dbg, h1_dram)
        p.dma("sync", ctx_dbg, ctx_dram)
        p.dma("sync", kin_dbg, k_in)
        p.dma("sync", vin_dbg, v_in)
        p.dma("sync", qsp_dbg, q_spill)
        p.dma("sync", nrm_dbg[0:1, 0:64], qn2[0:1, :])
        p.dma("sync", nrm_dbg[0:1, 64:96], kn2l[0:1, :])
        p.dma("sync", nrm_dbg[0:1, 96:128], kn2g[0:1, :])
    for sname in list(p.cnt.keys()):
        p.wait("sync", sname, p.cnt[sname])

    p.finalize_check()

    with nc.Block() as block:
        @block.tensor
        def _(e):
            for f in p.ops["tensor"]:
                f(e)

        @block.scalar
        def _(e):
            for f in p.ops["scalar"]:
                f(e)

        @block.vector
        def _(e):
            for f in p.ops["vector"]:
                f(e)

        @block.gpsimd
        def _(e):
            for f in p.ops["gpsimd"]:
                f(e)

        @block.sync
        def _(e):
            for f in p.ops["sync"]:
                f(e)
    return nc


def kernel(L0_attn_x, L0_attn_attn_qkvw, L0_attn_attn_qkvb, L0_attn_attn_ow,
           L0_attn_attn_ob, L0_ffn_gate_w, L0_ffn_up_w, L0_ffn_down_w,
           L0_ffn_fc1_w, L0_ffn_fc2_w, L0_ffn_expert_w, L0_ffn_ln_gamma,
           L0_ffn_ln_beta):
    x = np.ascontiguousarray(np.asarray(L0_attn_x, dtype=np.float32).reshape(S, D))
    weights = {
        "qkvw": np.ascontiguousarray(np.asarray(L0_attn_attn_qkvw, np.float32)),
        "qkvb": np.ascontiguousarray(np.asarray(L0_attn_attn_qkvb, np.float32)),
        "ow": np.ascontiguousarray(np.asarray(L0_attn_attn_ow, np.float32)),
        "ob": np.ascontiguousarray(np.asarray(L0_attn_attn_ob, np.float32)),
        "gate_w": np.ascontiguousarray(np.asarray(L0_ffn_gate_w, np.float32)),
        "up_w": np.ascontiguousarray(np.asarray(L0_ffn_up_w, np.float32)),
        "down_w": np.ascontiguousarray(np.asarray(L0_ffn_down_w, np.float32)),
        "fc1_w": np.ascontiguousarray(np.asarray(L0_ffn_fc1_w, np.float32)),
        "fc2_w": np.ascontiguousarray(np.asarray(L0_ffn_fc2_w, np.float32)),
        "expert_w": np.ascontiguousarray(np.asarray(L0_ffn_expert_w, np.float32)),
        "ln_g": np.ascontiguousarray(np.asarray(L0_ffn_ln_gamma, np.float32)),
        "ln_b": np.ascontiguousarray(np.asarray(L0_ffn_ln_beta, np.float32)),
    }
    nc = bass.Bass("TRN2", target_bir_lowering=False, num_devices=N_CORES)
    build(nc, DBG=DBG)
    in_maps = []
    for c in range(N_CORES):
        m = {"xc": np.ascontiguousarray(x[c * T:(c + 1) * T, :])}
        m.update(weights)
        in_maps.append(m)
    res = run_bass_kernel_spmd(nc, in_maps, core_ids=list(range(N_CORES)))
    out = np.concatenate([r["out"] for r in res.results], axis=0)
    if DBG:
        kernel._dbg = res.results
    return out.reshape(1, S, D)


# version 11 (final)
# speedup vs baseline: 1.0504x; 1.0308x over previous
import contextlib
import numpy as np
import ml_dtypes
import concourse.bass as bass
import concourse.mybir as mybir
from concourse.bass_utils import run_bass_kernel_spmd

F32 = mybir.dt.float32
BF16 = mybir.dt.bfloat16
AF = mybir.ActivationFunctionType
ALU = mybir.AluOpType
AX = mybir.AxisListType

N_CORES = 8
T = 1024
D = 4096
HEADS = 32
S = 8192
SCALE = 1.0 / float(np.sqrt(128.0))
DBG = False


def _pairs_mat():
    p = np.zeros((8, 28), np.float32)
    c = 0
    for i in range(8):
        for j in range(i + 1, 8):
            p[i, c] = 1.0
            p[j, c] = 1.0
            c += 1
    return p


class PB:
    ENGINES = ("tensor", "scalar", "vector", "gpsimd", "sync")

    def __init__(self, nc):
        self.nc = nc
        self.ops = {e: [] for e in self.ENGINES}
        self.sem = {}
        self.cnt = {}
        self.waited = {e: {} for e in self.ENGINES}
        self.wait_records = []

    def _sem(self, name):
        if name not in self.sem:
            self.sem[name] = self.nc.alloc_semaphore(name)
            self.cnt[name] = 0
        return self.sem[name]

    def wait(self, eng, sname, val):
        if val is None or val <= 0:
            return
        sem = self._sem(sname)
        w = self.waited[eng]
        if w.get(sname, -1) >= val:
            return
        w[sname] = val
        self.wait_records.append((eng, sname, val))
        self.ops[eng].append(lambda e, sem=sem, val=val: e.wait_ge(sem, val))

    def waitp(self, eng, pair):
        if pair:
            self.wait(eng, pair[0], pair[1])

    def op(self, eng, fn, inc=None, incn=1):
        if inc is None:
            self.ops[eng].append(lambda e, fn=fn: fn(e))
            return None
        sem = self._sem(inc)
        self.cnt[inc] += incn
        self.ops[eng].append(
            lambda e, fn=fn, sem=sem, n=incn: fn(e).then_inc(sem, n))
        return self.cnt[inc]

    def dma(self, eng, out_ap, in_ap, sem=None, **kw):
        sname = sem or ("g" if eng == "gpsimd" else "y")
        n = self.op(eng, lambda e, o=out_ap, i=in_ap, kw=kw: e.dma_start(out=o, in_=i, **kw),
                    inc=sname, incn=16)
        return (sname, n)

    def barrier(self):
        snap = dict(self.cnt)
        for eng in self.ENGINES:
            for sname, val in snap.items():
                self.wait(eng, sname, val)

    def finalize_check(self):
        for eng, sname, val in self.wait_records:
            assert val <= self.cnt.get(sname, 0), (
                f"{eng} waits {sname}>={val} but total incs = {self.cnt.get(sname, 0)}")


def build(nc, DBG=False):
    p = PB(nc)
    for base in ("t", "s", "v", "g", "y", "c1", "c2"):
        p._sem(base)
    xc = nc.dram_tensor("xc", [T, D], F32, kind="ExternalInput").ap()
    qkvw = nc.dram_tensor("qkvw", [D, 3 * D], F32, kind="ExternalInput").ap()
    qkvb = nc.dram_tensor("qkvb", [3 * D], F32, kind="ExternalInput").ap()
    ow = nc.dram_tensor("ow", [D, D], F32, kind="ExternalInput").ap()
    ob = nc.dram_tensor("ob", [D], F32, kind="ExternalInput").ap()
    gate_w = nc.dram_tensor("gate_w", [D, 8], F32, kind="ExternalInput").ap()
    up_w = nc.dram_tensor("up_w", [D, 8192], F32, kind="ExternalInput").ap()
    down_w = nc.dram_tensor("down_w", [8192, D], F32, kind="ExternalInput").ap()
    fc1_w = nc.dram_tensor("fc1_w", [D, 2048], F32, kind="ExternalInput").ap()
    fc2_w = nc.dram_tensor("fc2_w", [2048, D], F32, kind="ExternalInput").ap()
    expert_w = nc.dram_tensor("expert_w", [2048, 2048], F32, kind="ExternalInput").ap()
    ln_g = nc.dram_tensor("ln_g", [D], F32, kind="ExternalInput").ap()
    ln_b = nc.dram_tensor("ln_b", [D], F32, kind="ExternalInput").ap()
    out_d = nc.dram_tensor("out", [T, D], F32, kind="ExternalOutput").ap()

    q_spill = nc.dram_tensor("q_spill", [D, T], BF16, kind="Internal").ap()
    k_in = nc.dram_tensor("k_in", [D, T], BF16, kind="Internal").ap()
    v_in = nc.dram_tensor("v_in", [T, D], BF16, kind="Internal").ap()
    k_g = nc.dram_tensor("k_g", [N_CORES * D, T], BF16, kind="Internal",
                         addr_space="Shared").ap()
    v_g = nc.dram_tensor("v_g", [S, D], BF16, kind="Internal",
                         addr_space="Shared").ap()
    ctx_dram = nc.dram_tensor("ctx_dram", [D, T], BF16, kind="Internal").ap()
    x1_dram = nc.dram_tensor("x1_dram", [T, D], F32, kind="Internal").ap()
    h1_dram = nc.dram_tensor("h1_dram", [T, D], F32, kind="Internal").ap()
    h1fm_dram = nc.dram_tensor("h1fm_dram", [D, T], BF16, kind="Internal").ap()
    pre_dram = nc.dram_tensor("pre_dram", [T, D], F32, kind="Internal").ap()
    knl_d = nc.dram_tensor("knl_d", [1, 32], F32, kind="Internal").ap()
    kng_d = nc.dram_tensor("kng_d", [1, 32], F32, kind="Internal").ap()

    if DBG:
        x1_dbg = nc.dram_tensor("x1_dbg", [T, D], F32, kind="ExternalOutput").ap()
        h1_dbg = nc.dram_tensor("h1_dbg", [T, D], F32, kind="ExternalOutput").ap()
        ctx_dbg = nc.dram_tensor("ctx_dbg", [D, T], BF16, kind="ExternalOutput").ap()
        kin_dbg = nc.dram_tensor("kin_dbg", [D, T], BF16, kind="ExternalOutput").ap()
        vin_dbg = nc.dram_tensor("vin_dbg", [T, D], BF16, kind="ExternalOutput").ap()
        qsp_dbg = nc.dram_tensor("qsp_dbg", [D, T], BF16, kind="ExternalOutput").ap()
        nrm_dbg = nc.dram_tensor("nrm_dbg", [1, 128], F32, kind="ExternalOutput").ap()

    ident_d = nc.inline_tensor(np.eye(128, dtype=ml_dtypes.bfloat16), "ident_const").ap()
    pairs_d = nc.inline_tensor(_pairs_mat(), "pairs_const").ap()

    ones_f = nc.alloc_sbuf_tensor("ones_f", [128, 1], F32)
    ones_bf = nc.alloc_sbuf_tensor("ones_bf", [128, 1], BF16)
    onesr_f = nc.alloc_sbuf_tensor("onesr_f", [1, 128], F32)
    i_bf = nc.alloc_sbuf_tensor("i_bf", [128, 128], BF16)
    p_sb = nc.alloc_sbuf_tensor("p_sb", [8, 28], F32)
    bias_cols = nc.alloc_sbuf_tensor("bias_cols", [128, 96], F32)
    gate_sb = nc.alloc_sbuf_tensor("gate_sb", [128, 32 * 8], BF16)
    gsum_cols = nc.alloc_sbuf_tensor("gsum_cols", [128, 8], F32)
    cols = nc.alloc_sbuf_tensor("cols", [128, 32], F32)
    rowb = nc.alloc_sbuf_tensor("rowb", [1, 2 * 512], F32)
    bias_h = nc.alloc_sbuf_tensor("bias_h", [128, 2], F32)
    bias_const = nc.alloc_sbuf_tensor("bias_const", [128, 1], F32)
    scols = nc.alloc_sbuf_tensor("scols", [128, 8], F32)
    qn2 = nc.alloc_sbuf_tensor("qn2", [1, 64], F32)
    kn2l = nc.alloc_sbuf_tensor("kn2l", [1, 32], F32)
    kn2g = nc.alloc_sbuf_tensor("kn2g", [1, 32], F32)
    vfl = nc.alloc_sbuf_tensor("vfl", [128, 2], F32)
    sfl = nc.alloc_sbuf_tensor("sfl", [128, 2], F32)

    C_SUM, C_SQ, C_MU, C_MSQ, C_MU2, C_VAR, C_SD, C_RSTD = 0, 1, 2, 3, 4, 5, 6, 7
    C_MKA, C_MKB, C_MQ, C_C2, C_BV = 8, 9, 10, 11, 13
    C_SUMB, C_SQB = 14, 15

    def col(j):
        return cols[:, j:j + 1]

    def scol(j):
        return cols[0:1, j:j + 1]

    S_SD, S_SQ, S_C = 0, 1, 2

    def sco(j):
        return scols[:, j:j + 1]

    def vflush():
        return p.op("vector", lambda e: e.tensor_copy(vfl[:, 0:1], vfl[:, 1:2]), inc="v")

    def sflush():
        return p.op("scalar", lambda e: e.copy(sfl[:, 0:1], sfl[:, 1:2]), inc="s")

    def vchain(fn):
        n = p.op("vector", fn, inc="v")
        p.wait("vector", "v", n)
        return n

    def schain(fn):
        n = p.op("scalar", fn, inc="s")
        p.wait("scalar", "s", n)
        return n

    C_NT = 16

    def ln_stats(sumsq_ap):
        vchain(lambda e: e.tensor_scalar_mul(col(C_MU), col(C_SUM), 1.0 / D))
        vchain(lambda e, a=sumsq_ap: e.tensor_scalar_mul(col(C_MSQ), a, 1.0 / D))
        vchain(lambda e: e.tensor_tensor(col(C_MU2), col(C_MU), col(C_MU), ALU.mult))
        vchain(lambda e: e.tensor_tensor(col(C_VAR), col(C_MSQ), col(C_MU2), ALU.subtract))
        vchain(lambda e: e.tensor_scalar_add(col(C_SD), col(C_VAR), 1e-5))
        n = vflush()
        p.wait("scalar", "v", n)
        schain(lambda e: e.sqrt(sco(S_SD), col(C_SD)))
        m = sflush()
        p.wait("vector", "s", m)
        vchain(lambda e: e.reciprocal(col(C_RSTD), sco(S_SD)))
        r = C_RSTD
        for _ in range(2):
            r2 = C_NT if r == C_RSTD else C_RSTD
            vchain(lambda e, r=r: e.tensor_tensor(col(C_NT + 1), col(r), col(r), ALU.mult))
            vchain(lambda e: e.tensor_tensor(col(C_NT + 2), col(C_NT + 1), col(C_SD), ALU.mult))
            vchain(lambda e: e.tensor_scalar(col(C_NT + 1), col(C_NT + 2), -0.5, 1.5,
                                             ALU.mult, ALU.add))
            vchain(lambda e, r=r, r2=r2: e.tensor_tensor(col(r2), col(r), col(C_NT + 1), ALU.mult))
            r = r2
        assert r == C_RSTD
        return p.cnt["v"]

    p.dma("gpsimd", i_bf[:], ident_d, sem="big")
    p.dma("gpsimd", p_sb[:], pairs_d, sem="big")
    gw3 = gate_w.rearrange("(c p) e -> p c e", p=128)
    _, big_init = p.dma("gpsimd", gate_sb[:].rearrange("p (c e) -> p c e", e=8), gw3, sem="big")
    p.op("vector", lambda e: e.memset(ones_f[:], 1.0))
    p.op("vector", lambda e: e.memset(ones_bf[:], 1.0))
    p.op("vector", lambda e: e.memset(bias_const[:], -2.0))
    v_ones = p.op("vector", lambda e: e.memset(onesr_f[:], 1.0), inc="v")

    with nc.psum_tensor("pinit", [128, 512], F32) as pinit:
        s_last = 0
        t_need = 0
        for gidx in range(24):
            slot = gidx % 2
            rb = rowb[0:1, slot * 512:(slot + 1) * 512]
            p.wait("sync", "t", t_need)
            y_row = p.dma("sync", rb, qkvb[gidx * 512:(gidx + 1) * 512].rearrange("(a b) -> a b", a=1),
                          sem=f"rb{slot}")
            p.waitp("tensor", y_row)
            p.wait("tensor", "v", v_ones)
            p.wait("tensor", "s", s_last)
            for j in range(4):
                t_need = p.op("tensor", lambda e, rb=rb, j=j: e.matmul(
                    pinit[:, j:j + 1], rb[0:1, j * 128:(j + 1) * 128], ones_f[0:1, 0:1],
                    start=True, stop=True), inc="t")
            p.wait("scalar", "t", t_need)
            s_last = p.op("scalar", lambda e, gidx=gidx: e.copy(
                bias_cols[:, gidx * 4:(gidx + 1) * 4], pinit[:, 0:4]), inc="s")
        p.wait("tensor", "s", s_last)
        rowb_free_t = t_need
    p.barrier()

    og_order = list(range(32, 64)) + list(range(64, 96)) + list(range(0, 32))

    with (
        nc.sbuf_tensor("x_fm", [128, 32 * 1024], BF16) as x_fm,
        nc.sbuf_tensor("wbuf", [128, 2 * 32 * 512], BF16) as wbuf,
        nc.sbuf_tensor("x_tm", [128, 2 * 4096], BF16) as x_tm,
        nc.sbuf_tensor("stg", [128, 4 * 512], BF16) as stg,
        nc.sbuf_tensor("vstg", [128, 2 * 512], BF16) as vstg,
        nc.sbuf_tensor("nsq", [128, 2 * 512], BF16) as nsq,
        nc.psum_tensor("pnrm", [128, 1024], F32) as pnrm,
        nc.psum_tensor("pqa", [128, 1024], F32) as pqa,
        nc.psum_tensor("pqb", [128, 1024], F32) as pqb,
        nc.psum_tensor("ptr", [128, 2 * 512], F32) as ptr,
    ):
        tt_tdone = [0] * 8
        s_tr = 0
        for tt in range(8):
            slot = tt % 2
            if tt >= 2:
                p.wait("gpsimd", "t", tt_tdone[tt - 2])
            gl = p.dma("gpsimd", x_tm[:, slot * 4096:(slot + 1) * 4096],
                       xc[tt * 128:(tt + 1) * 128, :], sem=f"xl{slot}")
            p.waitp("tensor", gl)
            p.wait("tensor", "big", big_init)
            for fcg in range(8):
                pslot = ptr[:, (fcg % 2) * 512:(fcg % 2 + 1) * 512]
                p.wait("tensor", "s", s_tr)
                for j in range(4):
                    fc = fcg * 4 + j
                    tdone = p.op("tensor", lambda e, slot=slot, fc=fc, j=j, ps=pslot: e.matmul(
                        ps[:, j * 128:(j + 1) * 128],
                        x_tm[:, slot * 4096 + fc * 128: slot * 4096 + fc * 128 + 128],
                        i_bf[:], start=True, stop=True), inc="t")
                p.wait("scalar", "t", tdone)
                dst = x_fm[:].rearrange("q (c t) -> q c t", t=1024)[
                    :, fcg * 4:(fcg + 1) * 4, tt * 128:(tt + 1) * 128]
                src = pslot.rearrange("q (c t) -> q c t", t=128)
                s_tr = p.op("scalar", lambda e, d=dst, sp=src: e.copy(d, sp), inc="s")
            tt_tdone[tt] = tdone
        p.wait("tensor", "s", s_tr)

        nsq_free_t = [0, 0]
        pnrm_free_v = [0, 0]
        wslab_done = [0, 0]
        wslab_load = [None, None]
        slot_free_s = [0, 0, 0, 0]
        stg_free = [None] * 4
        vstg_free = [None, None]
        n_vstores = 0
        ptr_ev = s_tr
        psum_halves = [
            lambda: pqa[:, 0:512], lambda: pqa[:, 512:1024],
            lambda: pqb[:, 0:512], lambda: pqb[:, 512:1024],
        ]
        qkvw3 = qkvw.rearrange("(c p) o -> p c o", p=128)
        si = 0
        sgi = 0
        for grp in range(24):
            ogbase = og_order[grp * 4]
            pbuf = grp % 2
            p.wait("gpsimd", "t", wslab_done[pbuf])
            wdst = wbuf[:, pbuf * 16384:(pbuf + 1) * 16384].rearrange("p (c o) -> p c o", o=512)
            wsrc = qkvw3[:, :, ogbase * 128:ogbase * 128 + 512]
            wslab_load[pbuf] = p.dma("gpsimd", wdst, wsrc, sem=f"w{pbuf}")

            for og_in in range(4):
                og = ogbase + og_in
                for th in range(2):
                    half = psum_halves[si % 4]()
                    p.waitp("tensor", wslab_load[pbuf])
                    p.wait("tensor", "s", slot_free_s[si % 4])
                    for fc in range(32):
                        lt = wbuf[:, pbuf * 16384 + fc * 512 + og_in * 128:
                                  pbuf * 16384 + fc * 512 + og_in * 128 + 128]
                        rh = x_fm[:, fc * 1024 + th * 512: fc * 1024 + (th + 1) * 512]
                        tq = p.op("tensor", lambda e, h=half, lt=lt, rh=rh, f=fc: e.matmul(
                            h, lt, rh, start=(f == 0), stop=(f == 31)), inc="t")
                    wslab_done[pbuf] = tq
                    p.wait("scalar", "t", tq)
                    if og < 64:
                        sem_pfx = "sk" if og >= 32 else "sq"
                        p.waitp("scalar", stg_free[sgi % 4])
                        dstg = stg[:, (sgi % 4) * 512:(sgi % 4 + 1) * 512]
                        sev = p.op("scalar", lambda e, d=dstg, h=half, og=og: e.activation(
                            d, h, AF.Identity, bias=bias_cols[:, og:og + 1], scale=1.0),
                            inc="s")
                        slot_free_s[si % 4] = sev
                        p.wait("sync", "s", sev)
                        if og < 32:
                            tgt = q_spill[og * 128:(og + 1) * 128, th * 512:(th + 1) * 512]
                        else:
                            tgt = k_in[(og - 32) * 128:(og - 32) * 128 + 128, th * 512:(th + 1) * 512]
                        stg_free[sgi % 4] = p.dma("sync", tgt, dstg, sem=f"{sem_pfx}{sgi % 4}")
                        ns = sgi % 2
                        p.wait("scalar", "t", nsq_free_t[ns])
                        snq = p.op("scalar", lambda e, d=dstg, ns=ns: e.activation(
                            nsq[:, ns * 512:(ns + 1) * 512], d, AF.Square), inc="s")
                        p.wait("tensor", "s", snq)
                        p.wait("tensor", "v", pnrm_free_v[ns])
                        tnq = p.op("tensor", lambda e, ns=ns: e.matmul(
                            pnrm[0:1, ns * 512:(ns + 1) * 512], ones_bf[:],
                            nsq[:, ns * 512:(ns + 1) * 512], start=True, stop=True), inc="t")
                        nsq_free_t[ns] = tnq
                        p.wait("vector", "t", tnq)
                        if og < 32:
                            dstn = qn2[0:1, og * 2 + th: og * 2 + th + 1]
                            pnrm_free_v[ns] = p.op("vector", lambda e, d=dstn, ns=ns: e.tensor_reduce(
                                d, pnrm[0:1, ns * 512:(ns + 1) * 512], AX.X, ALU.max), inc="v")
                        else:
                            dstn = cols[0:1, 20 + th: 21 + th]
                            vr_ = p.op("vector", lambda e, d=dstn, ns=ns: e.tensor_reduce(
                                d, pnrm[0:1, ns * 512:(ns + 1) * 512], AX.X, ALU.max), inc="v")
                            pnrm_free_v[ns] = vr_
                            if th == 1:
                                p.wait("vector", "v", vr_)
                                vchain(lambda e, hh=og - 32: e.tensor_tensor(
                                    kn2l[0:1, hh:hh + 1], cols[0:1, 20:21],
                                    cols[0:1, 21:22], ALU.max))
                        sgi += 1
                    else:
                        vslot = n_vstores % 2
                        p.waitp("scalar", vstg_free[vslot])
                        vs1 = vstg[:, vslot * 512:(vslot + 1) * 512]
                        sev = p.op("scalar", lambda e, d=vs1, h=half, og=og: e.activation(
                            d, h, AF.Identity, bias=bias_cols[:, og:og + 1], scale=1.0),
                            inc="s")
                        slot_free_s[si % 4] = sev
                        p.wait("tensor", "s", sev)
                        ptslot = ptr[:, vslot * 512:(vslot + 1) * 512]
                        p.wait("tensor", "s", ptr_ev)
                        for j in range(4):
                            ttr = p.op("tensor", lambda e, ps=ptslot, j=j, vs=vs1: e.matmul(
                                ps[:, j * 128:(j + 1) * 128], vs[:, j * 128:(j + 1) * 128],
                                i_bf[:], start=True, stop=True), inc="t")
                        p.wait("scalar", "t", ttr)
                        ptr_ev = p.op("scalar", lambda e, d=vs1, ps=ptslot: e.copy(d, ps), inc="s")
                        p.wait("sync", "s", ptr_ev)
                        vdst = v_in[th * 512:(th + 1) * 512, (og - 64) * 128:(og - 64) * 128 + 128]
                        vdst3 = vdst.rearrange("(j q) d -> q j d", q=128)
                        vsrc3 = vs1.rearrange("q (j d) -> q j d", d=128)
                        vstg_free[vslot] = p.dma("sync", vdst3, vsrc3, sem=f"sv{vslot}")
                        n_vstores += 1
                    si += 1
            if grp == 15:
                for j in range(2):
                    p.wait("gpsimd", f"sv{j}", p.cnt.get(f"sv{j}", 0))
                p.op("gpsimd", lambda e: e.collective_compute(
                    "AllGather", ALU.bypass, replica_groups=[list(range(N_CORES))],
                    ins=[v_in], outs=[v_g]), inc="c2")
            if grp == 9:
                for j in range(4):
                    p.wait("gpsimd", f"sk{j}", p.cnt.get(f"sk{j}", 0))
                p.op("gpsimd", lambda e: e.collective_compute(
                    "AllGather", ALU.bypass, replica_groups=[list(range(N_CORES))],
                    ins=[k_in], outs=[k_g]), inc="c1")
                vkn = vflush()
                p.wait("sync", "v", vkn)
                ykn = p.dma("sync", knl_d, kn2l[0:1, :], sem="kn")
                p.waitp("gpsimd", ykn)
                p.op("gpsimd", lambda e: e.collective_compute(
                    "AllReduce", ALU.max, replica_groups=[list(range(N_CORES))],
                    ins=[knl_d], outs=[kng_d]), inc="c3")
    p.barrier()

    with (
        nc.sbuf_tensor("kbuf", [128, 2 * 8192], BF16) as kbuf,
        nc.sbuf_tensor("vbuf", [128, 2 * 8192], BF16) as vbuf,
        nc.sbuf_tensor("qb", [128, 2 * 512], BF16) as qb,
        nc.sbuf_tensor("prb", [128, 2 * 1024], BF16) as prb,
        nc.sbuf_tensor("pssum", [128, 512], BF16) as pssum,
        nc.sbuf_tensor("lpp", [128, 4 * 512], F32) as lpp,
        nc.sbuf_tensor("rcp", [1, 512], F32) as rcp,
        nc.sbuf_tensor("rbb", [128, 512], F32) as rbb,
        nc.sbuf_tensor("ctxs", [128, 2 * 512], BF16) as ctxs,
        nc.psum_tensor("pa", [128, 1024], F32) as pa,
        nc.psum_tensor("pb", [128, 1024], F32) as pb,
        nc.psum_tensor("pc", [128, 1024], F32) as pc,
        nc.psum_tensor("pd", [128, 512], F32) as pd,
        nc.psum_tensor("pbias", [128, 512], F32) as pbias,
    ):
        units = [(h, qw) for h in range(HEADS) for qw in range(2)]
        NU = len(units)
        kv_used_t = [0, 0]
        kv_load = [None, None]
        qb_free = [0, 0]
        ctxs_free = [None, None]
        pc_free = [0, 0]
        pd_guard_v = 0
        last_exp = [0, 0]
        last_pv = [0, 0]
        last_ps = [0, 0]
        bias_ready_s = [0] * NU
        yq_u = [None] * NU
        vl_u = [0] * NU
        pv31_u = [0] * NU
        tb_prev_t = 0
        pbias_free_s = [0, 0]
        pend_tail = None
        p.wait("sync", "c3", 1)
        ykng = p.dma("sync", kn2g[0:1, :], kng_d, sem="kn")

        def load_kv(h):
            buf2 = h % 2
            p.wait("sync", "t", kv_used_t[buf2])
            p.wait("sync", "c1", 1)
            p.wait("sync", "c2", 1)
            for r in range(N_CORES):
                ys = p.dma("sync", kbuf[:, buf2 * 8192 + r * 1024: buf2 * 8192 + (r + 1) * 1024],
                           k_g[r * D + h * 128: r * D + h * 128 + 128, :], sem=f"kv{buf2}")
            for r in range(N_CORES):
                src = v_g[r * 1024:(r + 1) * 1024, h * 128:(h + 1) * 128]
                src3 = src.rearrange("(a q) d -> q a d", q=128)
                dst3 = vbuf[:, buf2 * 8192 + r * 1024: buf2 * 8192 + (r + 1) * 1024].rearrange(
                    "q (a d) -> q a d", d=128)
                ys = p.dma("sync", dst3, src3, sem=f"kv{buf2}")
            kv_load[buf2] = ys

        def emit_qload(u):
            h, qw = units[u]
            p.wait("sync", "v", qb_free[qw])
            yq_u[u] = p.dma("sync", qb[:, qw * 512:(qw + 1) * 512],
                            q_spill[h * 128:(h + 1) * 128, qw * 512:(qw + 1) * 512],
                            sem=f"qb{qw}")

        def emit_bound(u):
            nonlocal tb_prev_t
            h, qw = units[u]
            p.waitp("vector", ykng)
            p.wait("vector", "t", tb_prev_t)
            vchain(lambda e, h=h, qw=qw: e.tensor_tensor(
                scol(C_C2), qn2[0:1, h * 2 + qw: h * 2 + qw + 1],
                kn2g[0:1, h:h + 1], ALU.mult))
            vv = vflush()
            p.wait("scalar", "v", vv)
            schain(lambda e: e.sqrt(scols[0:1, S_C:S_C + 1], scol(C_C2)))
            sv = sflush()
            p.wait("vector", "s", sv)
            vchain(lambda e: e.tensor_scalar(
                scol(C_BV), scols[0:1, S_C:S_C + 1], -SCALE, 12.0,
                ALU.mult, ALU.add))
            vv = vflush()
            p.wait("tensor", "v", vv)
            p.wait("tensor", "s", pbias_free_s[u % 2])
            tb = p.op("tensor", lambda e, u=u: e.matmul(
                pbias[:, (u % 2):(u % 2) + 1], onesr_f[0:1, :], scol(C_BV),
                start=True, stop=True), inc="t")
            tb_prev_t = tb
            p.wait("scalar", "t", tb)
            sb_ = p.op("scalar", lambda e, qw=qw, u=u: e.copy(
                bias_h[:, qw:qw + 1], pbias[:, (u % 2):(u % 2) + 1]), inc="s")
            bias_ready_s[u] = sb_
            pbias_free_s[u % 2] = sb_

        def emit_tail(u):
            nonlocal pd_guard_v
            h, qw = units[u]
            base = (u % 2) * 2
            pcs_t = pc[:, (u % 2) * 512:(u % 2 + 1) * 512]
            p.wait("tensor", "v", vl_u[u])
            p.wait("tensor", "v", pd_guard_v)
            tl = p.op("tensor", lambda e, base=base: e.matmul(
                pd[0:1, 0:512], ones_f[:], lpp[:, base * 512:(base + 1) * 512],
                start=True, stop=True), inc="t")
            p.wait("vector", "t", tl)
            vr = p.op("vector", lambda e: e.reciprocal(rcp[:], pd[0:1, 0:512]), inc="v")
            p.wait("tensor", "v", vr)
            tbc = p.op("tensor", lambda e: e.matmul(
                pd[:], onesr_f[0:1, :], rcp[0:1, :], start=True, stop=True), inc="t")
            p.wait("vector", "t", tbc)
            p.op("vector", lambda e: e.tensor_copy(rbb[:], pd[:]))
            p.wait("vector", "t", pv31_u[u])
            p.waitp("vector", ctxs_free[qw])
            vctx = p.op("vector", lambda e, qw=qw, pcs_t=pcs_t: e.tensor_tensor(
                ctxs[:, qw * 512:(qw + 1) * 512], pcs_t, rbb[:], ALU.mult), inc="v")
            pc_free[u % 2] = vctx
            pd_guard_v = vctx
            qb_free[qw] = vctx
            p.wait("sync", "v", vctx)
            ctxs_free[qw] = p.dma(
                "sync", ctx_dram[h * 128:(h + 1) * 128, qw * 512:(qw + 1) * 512],
                ctxs[:, qw * 512:(qw + 1) * 512], sem=f"cx{qw}")

        emit_qload(0)
        emit_bound(0)
        for u, (h, qw) in enumerate(units):
            buf = h % 2
            if qw == 0:
                if h == 0:
                    load_kv(0)
                    load_kv(1)
                elif h < HEADS - 1:
                    load_kv(h + 1)
                p.waitp("tensor", kv_load[buf])
            p.waitp("tensor", yq_u[u])
            base = (u % 2) * 2
            vl = p.op("vector", lambda e, base=base: e.memset(
                lpp[:, base * 512:(base + 1) * 512], 0.0), inc="v")
            pcs = pc[:, (u % 2) * 512:(u % 2 + 1) * 512]

            def emit_pv(pq, first_wait):
                parq = pq % 2
                p.wait("tensor", "s", last_exp[parq])
                if first_wait:
                    p.wait("tensor", "v", pc_free[u % 2])
                for jj in range(2):
                    kt = 2 * pq + jj
                    tpv = p.op("tensor", lambda e, jj=jj, kt=kt, buf=buf, parq=parq, pq=pq, pcs=pcs: e.matmul(
                        pcs, vbuf[:, buf * 8192 + kt * 128: buf * 8192 + kt * 128 + 128],
                        prb[:, parq * 1024 + jj * 512: parq * 1024 + (jj + 1) * 512],
                        start=(pq == 0 and jj == 0), stop=(pq == 31 and jj == 1),
                        skip_group_check=True), inc="t")
                last_pv[parq] = tpv

            p.wait("scalar", "s", bias_ready_s[u])
            for pp in range(32):
                par = pp % 2
                pair = pa if par == 0 else pb
                p.wait("tensor", "s", last_exp[par])
                for jj in range(2):
                    kt = 2 * pp + jj
                    tsc = p.op("tensor", lambda e, pr=pair, jj=jj, kt=kt, buf=buf, qw=qw: e.matmul(
                        pr[:, jj * 512:(jj + 1) * 512],
                        kbuf[:, buf * 8192 + kt * 128: buf * 8192 + kt * 128 + 128],
                        qb[:, qw * 512:(qw + 1) * 512],
                        start=True, stop=True), inc="t")
                p.wait("scalar", "t", tsc)
                p.wait("scalar", "t", last_pv[par])
                p.wait("scalar", "v", last_ps[par])
                se = p.op("scalar", lambda e, pr=pair, par=par, qw=qw: e.activation(
                    prb[:, par * 1024:(par + 1) * 1024], pr[:],
                    AF.Exp, bias=bias_h[:, qw:qw + 1], scale=SCALE), inc="s")
                last_exp[par] = se
                if pp >= 1:
                    emit_pv(pp - 1, pp == 1)
                p.wait("vector", "s", se)
                p.op("vector", lambda e, par=par: e.tensor_tensor(
                    pssum[:], prb[:, par * 1024:par * 1024 + 512],
                    prb[:, par * 1024 + 512:(par + 1) * 1024], ALU.add))
                vl = p.op("vector", lambda e, pp=pp, base=base: e.tensor_tensor(
                    lpp[:, (base + (pp + 1) % 2) * 512:(base + (pp + 1) % 2 + 1) * 512],
                    lpp[:, (base + pp % 2) * 512:(base + pp % 2) * 512 + 512],
                    pssum[:], ALU.add), inc="v")
                last_ps[par] = vl
                if pp == 2 and u + 1 < NU:
                    emit_bound(u + 1)
                if pp == 4 and pend_tail is not None:
                    emit_tail(pend_tail)
                    pend_tail = None
                if pp == 6 and u + 1 < NU:
                    emit_qload(u + 1)
            emit_pv(31, False)
            pv31_u[u] = last_pv[1]
            vl_u[u] = vl
            if qw == 1:
                kv_used_t[buf] = last_pv[1]
            pend_tail = u
        emit_tail(NU - 1)
    p.barrier()

    with (
        nc.sbuf_tensor("ctx_fm", [128, 32 * 1024], BF16) as ctx_fm,
        nc.sbuf_tensor("owbuf", [128, 2 * 32 * 512], BF16) as owbuf,
        nc.sbuf_tensor("ob_bc", [128, 4096], F32) as ob_bc,
        nc.sbuf_tensor("xres", [128, 2 * 512], F32) as xres,
        nc.sbuf_tensor("stga", [128, 2 * 512], F32) as stga,
        nc.sbuf_tensor("stgb", [128, 2 * 512], F32) as stgb,
        nc.psum_tensor("po", [128, 1024], F32) as po,
        nc.psum_tensor("pi4", [128, 512], F32) as pi4,
    ):
        t_ob = rowb_free_t
        s_ob = 0
        for gidx in range(8):
            slotr = gidx % 2
            rb = rowb[0:1, slotr * 512:(slotr + 1) * 512]
            p.wait("sync", "t", t_ob)
            yr = p.dma("sync", rb, ob[gidx * 512:(gidx + 1) * 512].rearrange("(a b) -> a b", a=1),
                       sem=f"rb{slotr}")
            p.waitp("tensor", yr)
            p.wait("tensor", "s", s_ob)
            t_ob = p.op("tensor", lambda e, rb=rb: e.matmul(
                pi4[:], onesr_f[0:1, :], rb, start=True, stop=True), inc="t")
            p.wait("scalar", "t", t_ob)
            s_ob = p.op("scalar", lambda e, gidx=gidx: e.copy(
                ob_bc[:, gidx * 512:(gidx + 1) * 512], pi4[:]), inc="s")
        rowb_free_t = t_ob
        p.wait("sync", "cx0", p.cnt.get("cx0", 0))
        p.wait("sync", "cx1", p.cnt.get("cx1", 0))
        ctx3 = ctx_dram.rearrange("(c p) t -> p c t", p=128)
        yctx = p.dma("sync", ctx_fm[:].rearrange("p (c t) -> p c t", t=1024), ctx3, sem="big")
        ow3 = ow.rearrange("(c p) o -> p c o", p=128)
        ow_done_t = [0, 0]
        ow_load = [None, None]
        po_free = [0, 0]
        xres_y = [None, None]
        xres_free = [0, 0]
        x1st_free = [None, None]
        for ofg in range(8):
            pbuf = ofg % 2
            p.wait("gpsimd", "t", ow_done_t[pbuf])
            wdst = owbuf[:, pbuf * 16384:(pbuf + 1) * 16384].rearrange("p (c o) -> p c o", o=512)
            ow_load[pbuf] = p.dma("gpsimd", wdst, ow3[:, :, ofg * 512:(ofg + 1) * 512],
                                  sem=f"w{pbuf}")
            for tt in range(8):
                pslot = po[:, (tt % 2) * 512:(tt % 2 + 1) * 512]
                p.waitp("tensor", ow_load[pbuf])
                p.waitp("tensor", yctx)
                p.wait("tensor", "v", po_free[tt % 2])
                p.wait("tensor", "s", s_ob)
                for cf in range(32):
                    tof = p.op("tensor", lambda e, ps=pslot, cf=cf, tt=tt, pbuf=pbuf: e.matmul(
                        ps, ctx_fm[:, cf * 1024 + tt * 128: cf * 1024 + tt * 128 + 128],
                        owbuf[:, pbuf * 16384 + cf * 512:pbuf * 16384 + (cf + 1) * 512],
                        start=(cf == 0), stop=(cf == 31)), inc="t")
                ow_done_t[pbuf] = tof
                xslot = tt % 2
                p.wait("sync", "v", xres_free[xslot])
                xres_y[xslot] = p.dma("sync", xres[:, xslot * 512:(xslot + 1) * 512],
                                      xc[tt * 128:(tt + 1) * 128, ofg * 512:(ofg + 1) * 512],
                                      sem=f"xr{xslot}")
                p.wait("vector", "t", tof)
                p.waitp("vector", xres_y[xslot])
                p.waitp("vector", x1st_free[tt % 2])
                p.op("vector", lambda e, ps=pslot, tt=tt, ofg=ofg: e.tensor_tensor(
                    stga[:, (tt % 2) * 512:(tt % 2 + 1) * 512], ps,
                    ob_bc[:, ofg * 512:(ofg + 1) * 512], ALU.add))
                vev = p.op("vector", lambda e, tt=tt, xslot=xslot: e.tensor_tensor(
                    stgb[:, (tt % 2) * 512:(tt % 2 + 1) * 512],
                    stga[:, (tt % 2) * 512:(tt % 2 + 1) * 512],
                    xres[:, xslot * 512:(xslot + 1) * 512], ALU.add), inc="v")
                po_free[tt % 2] = vev
                xres_free[xslot] = vev
                p.wait("sync", "v", vev)
                x1st_free[tt % 2] = p.dma(
                    "sync", x1_dram[tt * 128:(tt + 1) * 128, ofg * 512:(ofg + 1) * 512],
                    stgb[:, (tt % 2) * 512:(tt % 2 + 1) * 512], sem=f"xs{tt % 2}")
    p.barrier()

    with (
        nc.sbuf_tensor("h1_fm", [128, 32 * 1024], BF16) as h1_fm,
        nc.sbuf_tensor("x1t", [128, 4096], F32) as x1t,
        nc.sbuf_tensor("h1t", [128, 4096], F32) as h1t,
        nc.sbuf_tensor("h1bf", [128, 4096], BF16) as h1bf,
        nc.sbuf_tensor("sq4", [128, 4096], F32) as sq4,
        nc.psum_tensor("pt4", [128, 2 * 512], F32) as pt4,
    ):
        x1_cons_s = 0
        x1_cons_v = 0
        sq4_cons_v = 0
        h1bf_t = 0
        last_tr_s = 0
        h1t_st = None
        for tt in range(8):
            p.wait("sync", "s", x1_cons_s)
            p.wait("sync", "v", x1_cons_v)
            yx1 = p.dma("sync", x1t[:], x1_dram[tt * 128:(tt + 1) * 128, :], sem="xl0")
            p.waitp("scalar", yx1)
            p.wait("scalar", "v", sq4_cons_v)
            schain(lambda e: e.activation(sq4[:], x1t[:], AF.Square,
                                          accum_out=sco(S_SQ)))
            ssq = sflush()
            x1_cons_s = ssq
            p.waitp("vector", yx1)
            vchain(lambda e: e.tensor_reduce(col(C_SUM), x1t[:], AX.X, ALU.add))
            p.wait("vector", "s", ssq)
            sq4_cons_v = ln_stats(sco(S_SQ))
            p.waitp("vector", h1t_st)
            vap = p.op("vector", lambda e: e.tensor_scalar(
                h1t[:], x1t[:], col(C_MU), col(C_RSTD), ALU.subtract, ALU.mult), inc="v")
            p.wait("vector", "t", h1bf_t)
            v_apply = p.op("vector", lambda e: e.tensor_copy(h1bf[:], h1t[:]), inc="v")
            x1_cons_v = v_apply
            p.wait("sync", "v", v_apply)
            h1t_st = p.dma("sync", h1_dram[tt * 128:(tt + 1) * 128, :], h1t[:], sem="hs")
            p.wait("tensor", "v", v_apply)
            for fcg in range(8):
                ptsl = pt4[:, (fcg % 2) * 512:(fcg % 2 + 1) * 512]
                p.wait("tensor", "s", last_tr_s)
                for j in range(4):
                    fc = fcg * 4 + j
                    h1bf_t = p.op("tensor", lambda e, ps=ptsl, fc=fc, j=j: e.matmul(
                        ps[:, j * 128:(j + 1) * 128], h1bf[:, fc * 128:(fc + 1) * 128],
                        i_bf[:], start=True, stop=True), inc="t")
                p.wait("scalar", "t", h1bf_t)
                dst = h1_fm[:].rearrange("q (c t) -> q c t", t=1024)[
                    :, fcg * 4:(fcg + 1) * 4, tt * 128:(tt + 1) * 128]
                src = ptsl.rearrange("q (c t) -> q c t", t=128)
                last_tr_s = p.op("scalar", lambda e, d=dst, sp=src: e.copy(d, sp), inc="s")
        p.wait("sync", "s", last_tr_s)
        for fc in range(32):
            p.dma("sync", h1fm_dram[fc * 128:(fc + 1) * 128, :], h1_fm[:, fc * 1024:(fc + 1) * 1024])
        h1fm_y = p.cnt["y"]
    p.barrier()

    up3 = up_w.rearrange("(c p) o -> p c o", p=128)
    down3 = down_w.rearrange("(c p) o -> p c o", p=128)
    fc13 = fc1_w.rearrange("(c p) o -> p c o", p=128)
    fc23 = fc2_w.rearrange("(c p) o -> p c o", p=128)
    ex3 = expert_w.rearrange("(c p) o -> p c o", p=128)
    h1fm3 = h1fm_dram.rearrange("(c p) t -> p c t", p=128)

    with (
        nc.sbuf_tensor("hfm", [128, 64 * 512], BF16) as hfm,
        nc.sbuf_tensor("wb5", [128, 2 * 8192], BF16) as wb5,
        nc.sbuf_tensor("efm", [128, 16 * 512], BF16) as efm,
        nc.sbuf_tensor("g5", [8, 512], F32) as g5,
        nc.sbuf_tensor("rl5", [128, 512], F32) as rl5,
        nc.sbuf_tensor("ssc", [128, 512], F32) as ssc,
        nc.sbuf_tensor("ost", [128, 2 * 512], F32) as ost,
        nc.psum_tensor("pu", [128, 1024], F32) as pu,
        nc.psum_tensor("pu2", [128, 1024], F32) as pu2,
        nc.psum_tensor("pf2", [128, 512], F32) as pf2,
        nc.psum_tensor("pg5", [128, 512], F32) as pg5,
    ):
        p_ttl = [lambda: pu[:, 0:512], lambda: pu[:, 512:1024],
                 lambda: pu2[:, 0:512], lambda: pu2[:, 512:1024]]
        wb5_free_t = [0, 0]
        fc2b_free_t = [0, 0]
        dnx_free_t = [0, 0]
        pslot_free_s = [0, 0, 0, 0]
        pf2_free_v = 0
        ssc_free_v = 0
        ost_free = [None, None]
        g5_cons_t = 0
        scols_cons_carry = 0
        epi_done_v = 0
        hfm_cons_t = 0
        efm_cons_t = 0
        h1w_cons_t = 0
        for w in range(2):
            with nc.sbuf_tensor(f"h1w{w}", [128, 32 * 512], BF16) as h1w:
                p.wait("sync", "y", h1fm_y)
                p.wait("sync", "t", h1w_cons_t)
                p.wait("sync", "v", epi_done_v)
                yh1w = p.dma("sync", h1w[:].rearrange("p (c t) -> p c t", t=512),
                             h1fm3[:, :, w * 512:(w + 1) * 512], sem="big")
                slotpu = 0
                wl = None
                pb5 = 0
                for lc in range(16):
                    if lc % 2 == 0:
                        pb5 = (lc // 2) % 2
                        p.wait("gpsimd", "t", wb5_free_t[pb5])
                        wdst = wb5[:, pb5 * 8192:(pb5 + 1) * 8192].rearrange("p (c o) -> p c o", o=256)
                        wl = p.dma("gpsimd", wdst, fc13[:, :, (lc // 2) * 256:(lc // 2) * 256 + 256],
                                   sem=f"w{pb5}")
                    sl = slotpu % 4
                    ps = p_ttl[sl]()
                    p.waitp("tensor", wl)
                    p.waitp("tensor", yh1w)
                    p.wait("tensor", "s", pslot_free_s[sl])
                    for fc in range(32):
                        lt = wb5[:, pb5 * 8192 + fc * 256 + (lc % 2) * 128:
                                 pb5 * 8192 + fc * 256 + (lc % 2) * 128 + 128]
                        tfc1 = p.op("tensor", lambda e, ps=ps, lt=lt, fc=fc: e.matmul(
                            ps, lt, h1w[:, fc * 512:(fc + 1) * 512],
                            start=(fc == 0), stop=(fc == 31)), inc="t")
                    wb5_free_t[pb5] = tfc1
                    p.wait("scalar", "t", tfc1)
                    p.wait("scalar", "t", hfm_cons_t)
                    pslot_free_s[sl] = p.op("scalar", lambda e, lc=lc, ps=ps: e.copy(
                        hfm[:, lc * 512:(lc + 1) * 512], ps), inc="s")
                    slotpu += 1
                lat_ready_s = max(pslot_free_s)
                for mc in range(16):
                    if mc % 2 == 0:
                        pb5 = (mc // 2) % 2
                        p.wait("gpsimd", "t", wb5_free_t[pb5])
                        wdst = wb5[:, pb5 * 8192:pb5 * 8192 + 4096].rearrange("p (c o) -> p c o", o=256)
                        wl = p.dma("gpsimd", wdst, ex3[:, :, (mc // 2) * 256:(mc // 2) * 256 + 256],
                                   sem=f"w{pb5}")
                    sl = slotpu % 4
                    ps = p_ttl[sl]()
                    p.waitp("tensor", wl)
                    p.wait("tensor", "s", lat_ready_s)
                    p.wait("tensor", "s", pslot_free_s[sl])
                    for lc in range(16):
                        lt = wb5[:, pb5 * 8192 + lc * 256 + (mc % 2) * 128:
                                 pb5 * 8192 + lc * 256 + (mc % 2) * 128 + 128]
                        tex = p.op("tensor", lambda e, ps=ps, lt=lt, lc=lc: e.matmul(
                            ps, lt, hfm[:, lc * 512:(lc + 1) * 512],
                            start=(lc == 0), stop=(lc == 15)), inc="t")
                    wb5_free_t[pb5] = tex
                    p.wait("scalar", "t", tex)
                    p.wait("scalar", "t", efm_cons_t)
                    pslot_free_s[sl] = p.op("scalar", lambda e, mc=mc, ps=ps: e.copy(
                        efm[:, mc * 512:(mc + 1) * 512], ps), inc="s")
                    slotpu += 1
                efm_ready_s = max(pslot_free_s)
                lat_last_read_t = wb5_free_t[pb5]
                p.waitp("tensor", yh1w)
                p.wait("tensor", "big", big_init)
                p.wait("tensor", "s", g5_cons_t * 0)
                for fc in range(32):
                    tg = p.op("tensor", lambda e, fc=fc: e.matmul(
                        pg5[0:8, :], gate_sb[:, fc * 8:(fc + 1) * 8], h1w[:, fc * 512:(fc + 1) * 512],
                        start=(fc == 0), stop=(fc == 31)), inc="t")
                p.wait("scalar", "t", tg)
                p.op("scalar", lambda e: e.copy(g5[:], pg5[0:8, :]))
                sg5 = sflush()
                vgs = 0
                for ttl in range(4):
                    p.wait("tensor", "s", sg5)
                    p.wait("tensor", "v", vgs)
                    p.wait("tensor", "v", pf2_free_v)
                    tpp = p.op("tensor", lambda e, ttl=ttl: e.matmul(
                        pf2[:, 0:28], g5[0:8, ttl * 128:(ttl + 1) * 128], p_sb[:],
                        start=True, stop=True), inc="t")
                    p.wait("vector", "t", tpp)
                    vgs = p.op("vector", lambda e, w=w, ttl=ttl: e.tensor_reduce(
                        gsum_cols[:, w * 4 + ttl: w * 4 + ttl + 1], pf2[:, 0:28],
                        AX.X, ALU.max), inc="v")
                pf2_free_v = vgs
                gs_ready_v = vgs
                g5_cons_t = tpp
                rl_free_s = 0
                for ic in range(64):
                    if ic % 2 == 0:
                        pb5 = (ic // 2) % 2
                        p.wait("gpsimd", "t", wb5_free_t[pb5])
                        wdst = wb5[:, pb5 * 8192:(pb5 + 1) * 8192].rearrange("p (c o) -> p c o", o=256)
                        wl = p.dma("gpsimd", wdst, up3[:, :, (ic // 2) * 256:(ic // 2) * 256 + 256],
                                   sem=f"w{pb5}")
                    sl = slotpu % 4
                    ps = p_ttl[sl]()
                    p.waitp("tensor", wl)
                    p.wait("tensor", "s", pslot_free_s[sl])
                    for fc in range(32):
                        lt = wb5[:, pb5 * 8192 + fc * 256 + (ic % 2) * 128:
                                 pb5 * 8192 + fc * 256 + (ic % 2) * 128 + 128]
                        tup = p.op("tensor", lambda e, ps=ps, lt=lt, fc=fc: e.matmul(
                            ps, lt, h1w[:, fc * 512:(fc + 1) * 512],
                            start=(fc == 0), stop=(fc == 31)), inc="t")
                    wb5_free_t[pb5] = tup
                    p.wait("scalar", "t", tup)
                    p.wait("scalar", "s", rl_free_s)
                    if ic < 16:
                        p.wait("scalar", "t", lat_last_read_t)
                    p.op("scalar", lambda e, ps=ps: e.activation(rl5[:], ps, AF.Relu))
                    pslot_free_s[sl] = p.op("scalar", lambda e, ic=ic: e.activation(
                        hfm[:, ic * 512:(ic + 1) * 512], rl5[:], AF.Square), inc="s")
                    rl_free_s = pslot_free_s[sl]
                    slotpu += 1
                hfm_ready_s = max(pslot_free_s)
                h1w_cons_t = max(wb5_free_t + [tg])
            with (
                nc.sbuf_tensor(f"downx{w}", [128, 2 * 8192], BF16) as dnx,
                nc.sbuf_tensor(f"fc2b{w}", [128, 2 * 8192], BF16) as fc2b,
            ):
                pttl_stop_t = [0, 0, 0, 0]
                for hfg in range(8):
                    fcb = hfg % 2
                    p.wait("gpsimd", "t", fc2b_free_t[fcb])
                    wdst = fc2b[:, fcb * 8192:(fcb + 1) * 8192].rearrange("p (c o) -> p c o", o=512)
                    yfc2 = p.dma("gpsimd", wdst, fc23[:, :, hfg * 512:(hfg + 1) * 512],
                                 sem=f"f2{fcb}")
                    qslots = [wb5[:, 0:8192], wb5[:, 8192:16384], dnx[:, 0:8192], dnx[:, 8192:16384]]
                    yq4 = [None] * 4
                    for iq in range(4):
                        if iq < 2:
                            p.wait("gpsimd", "t", wb5_free_t[iq])
                        else:
                            p.wait("gpsimd", "t", dnx_free_t[iq - 2])
                            p.wait("gpsimd", "t", h1w_cons_t)
                        wdst2 = qslots[iq].rearrange("p (c o) -> p c o", o=512)
                        semn = f"w{iq}" if iq < 2 else f"dn{iq - 2}"
                        yq4[iq] = p.dma("gpsimd", wdst2,
                                        down3[:, iq * 16:(iq + 1) * 16, hfg * 512:(hfg + 1) * 512],
                                        sem=semn)
                    tdn = 0
                    for iq in range(4):
                        p.waitp("tensor", yq4[iq])
                        if iq == 0:
                            p.wait("tensor", "s", hfm_ready_s)
                            for ttl in range(4):
                                p.wait("tensor", "s", pslot_free_s[ttl])
                        for ic in range(16):
                            i_ch = iq * 16 + ic
                            for ttl in range(4):
                                tdn = p.op("tensor", lambda e, iq=iq, ic=ic, ttl=ttl, i_ch=i_ch: e.matmul(
                                    p_ttl[ttl](),
                                    hfm[:, i_ch * 512 + ttl * 128: i_ch * 512 + ttl * 128 + 128],
                                    qslots[iq][:, ic * 512:(ic + 1) * 512],
                                    start=(i_ch == 0), stop=(i_ch == 63),
                                    skip_group_check=True), inc="t")
                                if i_ch == 63:
                                    pttl_stop_t[ttl] = tdn
                        if iq < 2:
                            wb5_free_t[iq] = tdn
                        else:
                            dnx_free_t[iq - 2] = tdn
                    hfm_cons_t = tdn
                    for ttl in range(4):
                        p.waitp("tensor", yfc2)
                        p.wait("tensor", "s", efm_ready_s)
                        p.wait("tensor", "v", pf2_free_v)
                        for mc in range(16):
                            tf2 = p.op("tensor", lambda e, mc=mc, ttl=ttl, fcb=fcb: e.matmul(
                                pf2[:], efm[:, mc * 512 + ttl * 128: mc * 512 + ttl * 128 + 128],
                                fc2b[:, fcb * 8192 + mc * 512: fcb * 8192 + (mc + 1) * 512],
                                start=(mc == 0), stop=(mc == 15)), inc="t")
                        fc2b_free_t[fcb] = tf2
                        efm_cons_t = tf2
                        p.wait("scalar", "t", pttl_stop_t[ttl])
                        p.wait("scalar", "v", ssc_free_v)
                        sdc = p.op("scalar", lambda e, ttl=ttl: e.copy(ssc[:], p_ttl[ttl]()), inc="s")
                        pslot_free_s[ttl] = sdc
                        oslot = (hfg * 4 + ttl) % 2
                        p.wait("vector", "t", tf2)
                        p.wait("vector", "s", sdc)
                        p.wait("vector", "v", gs_ready_v)
                        p.waitp("vector", ost_free[oslot])
                        vpre = p.op("vector", lambda e, w=w, ttl=ttl, oslot=oslot: e.scalar_tensor_tensor(
                            ost[:, oslot * 512:(oslot + 1) * 512], pf2[:],
                            gsum_cols[:, w * 4 + ttl: w * 4 + ttl + 1], ssc[:],
                            ALU.mult, ALU.add), inc="v")
                        pf2_free_v = vpre
                        ssc_free_v = vpre
                        p.wait("sync", "v", vpre)
                        ost_free[oslot] = p.dma(
                            "sync",
                            pre_dram[w * 512 + ttl * 128: w * 512 + (ttl + 1) * 128,
                                     hfg * 512:(hfg + 1) * 512],
                            ost[:, oslot * 512:(oslot + 1) * 512], sem=f"os{oslot}")
            with contextlib.ExitStack() as _epi:
                gbc = _epi.enter_context(nc.sbuf_tensor(f"gbc{w}", [128, 4096], F32))
                bbc = _epi.enter_context(nc.sbuf_tensor(f"bbc{w}", [128, 4096], F32))
                rch = _epi.enter_context(nc.sbuf_tensor(f"rch{w}", [128, 2 * 512], F32))
                sch = _epi.enter_context(nc.sbuf_tensor(f"sch{w}", [128, 2 * 512], F32))
                h1r = _epi.enter_context(nc.sbuf_tensor(f"h1r{w}", [128, 2 * 512], F32))
                z_sb = _epi.enter_context(nc.sbuf_tensor(f"zsb{w}", [128, 4096], F32))
                pe5 = _epi.enter_context(nc.psum_tensor(f"pe5{w}", [128, 512], F32))
                s_bc = 0
                t_bc = rowb_free_t
                for which, (src_ap, dstt) in enumerate(((ln_g, gbc), (ln_b, bbc))):
                    for gidx in range(8):
                        slotr = gidx % 2
                        rb = rowb[0:1, slotr * 512:(slotr + 1) * 512]
                        p.wait("sync", "t", t_bc)
                        yr = p.dma("sync", rb,
                                   src_ap[gidx * 512:(gidx + 1) * 512].rearrange("(a b) -> a b", a=1),
                                   sem=f"rb{slotr}")
                        p.waitp("tensor", yr)
                        p.wait("tensor", "s", s_bc)
                        p.wait("tensor", "t", hfm_cons_t)
                        t_bc = p.op("tensor", lambda e, rb=rb: e.matmul(
                            pe5[:], onesr_f[0:1, :], rb, start=True, stop=True), inc="t")
                        p.wait("scalar", "t", t_bc)
                        s_bc = p.op("scalar", lambda e, d=dstt, gidx=gidx: e.copy(
                            d[:, gidx * 512:(gidx + 1) * 512], pe5[:]), inc="s")
                rowb_free_t = t_bc
                bc_ready = s_bc
                last_v = 0
                scols_cons = scols_cons_carry
                for ttl in range(4):
                    trow = w * 512 + ttl * 128
                    p.wait("sync", "os0", p.cnt.get("os0", 0))
                    p.wait("sync", "os1", p.cnt.get("os1", 0))
                    rload = [None, None]
                    rfree = [0, 0]
                    ssq_all = 0
                    for c in range(8):
                        sl = c % 2
                        p.wait("sync", "v", rfree[sl])
                        rload[sl] = p.dma("sync", rch[:, sl * 512:(sl + 1) * 512],
                                          pre_dram[trow:trow + 128, c * 512:(c + 1) * 512],
                                          sem=f"rc{sl}")
                        p.waitp("scalar", rload[sl])
                        p.wait("scalar", "v", scols_cons)
                        fn = lambda e, sl=sl, c=c: e.activation(
                            rl5[:], rch[:, sl * 512:(sl + 1) * 512], AF.Square,
                            accum_out=scols[:, c:c + 1])
                        if c == 7:
                            schain(fn)
                            ssq_all = sflush()
                        else:
                            p.op("scalar", fn)
                        p.waitp("vector", rload[sl])
                        rfree[sl] = p.op("vector", lambda e, sl=sl, c=c: e.tensor_reduce(
                            cols[:, 16 + c:17 + c], rch[:, sl * 512:(sl + 1) * 512],
                            AX.X, ALU.add), inc="v")
                    p.wait("vector", "v", rfree[1])
                    vchain(lambda e: e.tensor_reduce(col(C_SUM), cols[:, 16:24], AX.X, ALU.add))
                    p.wait("vector", "s", ssq_all)
                    vchain(lambda e: e.tensor_reduce(col(C_SQ), scols[:, 0:8], AX.X, ALU.add))
                    scols_cons = ln_stats(col(C_SQ))
                    hfree = [0, 0]
                    ssq_all = 0
                    for c in range(8):
                        sl = c % 2
                        p.wait("sync", "v", rfree[sl])
                        rload[sl] = p.dma("sync", rch[:, sl * 512:(sl + 1) * 512],
                                          pre_dram[trow:trow + 128, c * 512:(c + 1) * 512],
                                          sem=f"rc{sl}")
                        p.wait("sync", "v", hfree[sl])
                        yh = p.dma("sync", h1r[:, sl * 512:(sl + 1) * 512],
                                   h1_dram[trow:trow + 128, c * 512:(c + 1) * 512],
                                   sem=f"hr{sl}")
                        p.waitp("vector", yh)
                        p.waitp("vector", rload[sl])
                        p.wait("vector", "s", bc_ready)
                        p.op("vector", lambda e, sl=sl: e.tensor_scalar(
                            sch[:, 0:512], rch[:, sl * 512:(sl + 1) * 512],
                            col(C_MU), col(C_RSTD), ALU.subtract, ALU.mult))
                        p.op("vector", lambda e, c=c: e.tensor_tensor(
                            sch[:, 512:1024], sch[:, 0:512], gbc[:, c * 512:(c + 1) * 512], ALU.mult))
                        p.op("vector", lambda e, c=c: e.tensor_tensor(
                            sch[:, 0:512], sch[:, 512:1024], bbc[:, c * 512:(c + 1) * 512], ALU.add))
                        vz = vchain(lambda e, sl=sl, c=c: e.tensor_tensor(
                            z_sb[:, c * 512:(c + 1) * 512], sch[:, 0:512],
                            h1r[:, sl * 512:(sl + 1) * 512], ALU.add))
                        hfree[sl] = vz
                        vzr = p.op("vector", lambda e, c=c: e.tensor_reduce(
                            cols[:, 16 + c:17 + c], z_sb[:, c * 512:(c + 1) * 512],
                            AX.X, ALU.add), inc="v")
                        rfree[sl] = vzr
                        p.wait("scalar", "v", vz)
                        fn = lambda e, c=c: e.activation(
                            rl5[:], z_sb[:, c * 512:(c + 1) * 512], AF.Square,
                            accum_out=scols[:, c:c + 1])
                        if c == 7:
                            schain(fn)
                            ssq_all = sflush()
                        else:
                            p.op("scalar", fn)
                    p.wait("vector", "v", rfree[1])
                    vchain(lambda e: e.tensor_reduce(col(C_SUM), cols[:, 16:24], AX.X, ALU.add))
                    p.wait("vector", "s", ssq_all)
                    vchain(lambda e: e.tensor_reduce(col(C_SQ), scols[:, 0:8], AX.X, ALU.add))
                    scols_cons = ln_stats(col(C_SQ))
                    ofree = [None, None]
                    for c in range(8):
                        sl = c % 2
                        p.waitp("vector", ofree[sl])
                        vo = p.op("vector", lambda e, sl=sl, c=c: e.tensor_scalar(
                            ost[:, sl * 512:(sl + 1) * 512], z_sb[:, c * 512:(c + 1) * 512],
                            col(C_MU), col(C_RSTD), ALU.subtract, ALU.mult), inc="v")
                        last_v = vo
                        p.wait("sync", "v", vo)
                        ofree[sl] = p.dma("sync", out_d[trow:trow + 128, c * 512:(c + 1) * 512],
                                          ost[:, sl * 512:(sl + 1) * 512], sem=f"ot{sl}")
                epi_done_v = last_v
                scols_cons_carry = scols_cons
            p.barrier()

    if DBG:
        for sname in list(p.cnt.keys()):
            p.wait("sync", sname, p.cnt[sname])
        p.dma("sync", x1_dbg, x1_dram)
        p.dma("sync", h1_dbg, h1_dram)
        p.dma("sync", ctx_dbg, ctx_dram)
        p.dma("sync", kin_dbg, k_in)
        p.dma("sync", vin_dbg, v_in)
        p.dma("sync", qsp_dbg, q_spill)
        p.dma("sync", nrm_dbg[0:1, 0:64], qn2[0:1, :])
        p.dma("sync", nrm_dbg[0:1, 64:96], kn2l[0:1, :])
        p.dma("sync", nrm_dbg[0:1, 96:128], kn2g[0:1, :])
    for sname in list(p.cnt.keys()):
        p.wait("sync", sname, p.cnt[sname])

    p.finalize_check()

    with nc.Block() as block:
        @block.tensor
        def _(e):
            for f in p.ops["tensor"]:
                f(e)

        @block.scalar
        def _(e):
            for f in p.ops["scalar"]:
                f(e)

        @block.vector
        def _(e):
            for f in p.ops["vector"]:
                f(e)

        @block.gpsimd
        def _(e):
            for f in p.ops["gpsimd"]:
                f(e)

        @block.sync
        def _(e):
            for f in p.ops["sync"]:
                f(e)
    return nc


def kernel(L0_attn_x, L0_attn_attn_qkvw, L0_attn_attn_qkvb, L0_attn_attn_ow,
           L0_attn_attn_ob, L0_ffn_gate_w, L0_ffn_up_w, L0_ffn_down_w,
           L0_ffn_fc1_w, L0_ffn_fc2_w, L0_ffn_expert_w, L0_ffn_ln_gamma,
           L0_ffn_ln_beta):
    x = np.ascontiguousarray(np.asarray(L0_attn_x, dtype=np.float32).reshape(S, D))
    weights = {
        "qkvw": np.ascontiguousarray(np.asarray(L0_attn_attn_qkvw, np.float32)),
        "qkvb": np.ascontiguousarray(np.asarray(L0_attn_attn_qkvb, np.float32)),
        "ow": np.ascontiguousarray(np.asarray(L0_attn_attn_ow, np.float32)),
        "ob": np.ascontiguousarray(np.asarray(L0_attn_attn_ob, np.float32)),
        "gate_w": np.ascontiguousarray(np.asarray(L0_ffn_gate_w, np.float32)),
        "up_w": np.ascontiguousarray(np.asarray(L0_ffn_up_w, np.float32)),
        "down_w": np.ascontiguousarray(np.asarray(L0_ffn_down_w, np.float32)),
        "fc1_w": np.ascontiguousarray(np.asarray(L0_ffn_fc1_w, np.float32)),
        "fc2_w": np.ascontiguousarray(np.asarray(L0_ffn_fc2_w, np.float32)),
        "expert_w": np.ascontiguousarray(np.asarray(L0_ffn_expert_w, np.float32)),
        "ln_g": np.ascontiguousarray(np.asarray(L0_ffn_ln_gamma, np.float32)),
        "ln_b": np.ascontiguousarray(np.asarray(L0_ffn_ln_beta, np.float32)),
    }
    nc = bass.Bass("TRN2", target_bir_lowering=False, num_devices=N_CORES)
    build(nc, DBG=DBG)
    in_maps = []
    for c in range(N_CORES):
        m = {"xc": np.ascontiguousarray(x[c * T:(c + 1) * T, :])}
        m.update(weights)
        in_maps.append(m)
    res = run_bass_kernel_spmd(nc, in_maps, core_ids=list(range(N_CORES)))
    out = np.concatenate([r["out"] for r in res.results], axis=0)
    if DBG:
        kernel._dbg = res.results
    return out.reshape(1, S, D)
